# version 19
# baseline (speedup 1.0000x reference)
import math
import numpy as np
import concourse.bass as bass
import concourse.mybir as mybir
from concourse.bass_utils import run_bass_kernel_spmd
from contextlib import ExitStack

F32 = mybir.dt.float32
F16 = mybir.dt.float16
U8 = mybir.dt.uint8
AF = mybir.ActivationFunctionType
ALU = mybir.AluOpType
AX = mybir.AxisListType

FULL_CFG = dict(ROWS=1024, H=4096, I=11008, E=8, RK=64, K=128, ICH=512,
                KGS=8, NIT=28, S_TOT=2048, KAPPA=5056, PACE=6, PD=2, KGS2=8)
TINY_CFG = dict(ROWS=256, H=512, I=1024, E=8, RK=64, K=128, ICH=256,
                KGS=4, NIT=24, S_TOT=512, KAPPA=384, PACE=5, PD=4)


def derive(cfg):
    d = dict(cfg)
    d["T"] = cfg["ROWS"] // 128
    d["HB"] = cfg["H"] // 128
    ich, I = cfg["ICH"], cfg["I"]
    d["NIC"] = (I + ich - 1) // ich
    d["ICW"] = [min(ich, I - c * ich) for c in range(d["NIC"])]
    d["KB"] = I // 128
    kgs = cfg["KGS"]
    d["NKG"] = (d["KB"] + kgs - 1) // kgs
    d["KGW"] = [min(kgs, d["KB"] - g * kgs) for g in range(d["NKG"])]
    d["NHC"] = cfg["H"] // 512
    d["ER"] = cfg["E"] * cfg["RK"]
    d["ERB"] = d["ER"] // 128
    d["TK"] = I // 2
    npp = max(1, (d["KB"] + 21) // 22)
    base = d["KB"] // npp
    rem = d["KB"] - base * npp
    d["PB"] = [base + (1 if i < rem else 0) for i in range(npp)]
    d["PBMAX"] = max(d["PB"])
    return d


class Sem:
    __slots__ = ("h", "n")

    def __init__(self, handle):
        self.h = handle
        self.n = 0


class Eng:
    def __init__(self, name, real, sem):
        self.name = name
        self.real = real
        self.sem = sem
        self.seen = {}

    def wait(self, sem, val):
        if val <= 0:
            return
        k = id(sem)
        if self.seen.get(k, -1) >= val:
            return
        self.seen[k] = val
        if self.real is not None:
            self.real.wait_ge(sem.h, val)

    def chain(self):
        self.wait(self.sem, self.sem.n)

    def op(self, method, *args, inc=None, mark=None, w=None, **kw):
        incs = []
        if inc is True:
            incs = [(self.sem, 1)]
        elif inc:
            incs = inc
        if self.real is not None:
            inst = getattr(self.real, method)(*args, **kw)
            for (s, k) in incs:
                inst.then_inc(s.h, k)
        for (s, k) in incs:
            s.n += k
        if mark is not None:
            s, _ = incs[-1]
            w.prod[mark] = (s, s.n)

    def need(self, w, key):
        s, v = w.prod[key]
        self.wait(s, v)


class Walk:
    def mark_here(self, key, semname):
        s = self.sems[semname]
        self.prod[key] = (s, s.n)

    def __init__(self, scalar_sems, list_sems, active, engines):
        self.prod = {}
        self.sems = {k: Sem(v) for k, v in scalar_sems.items()}
        for k, lst in list_sems.items():
            self.sems[k] = [Sem(x) for x in lst]
        self.E = {}
        own_of = {"vector": "cV", "scalar": "cS", "tensor": "cT",
                  "sync": "cY", "gpsimd": "cG"}
        for name in ["vector", "scalar", "tensor", "sync", "gpsimd"]:
            self.E[name] = Eng(name, engines.get(name), self.sems[own_of[name]])


def schedule(w, t_, d):
    V = w.E["vector"]; S = w.E["scalar"]; P = w.E["tensor"]
    Y = w.E["sync"]; G = w.E["gpsimd"]
    sm = w.sems
    T, HB, NIC, ICW, KB, NKG, KGW, NHC, ER, ERB, TK, NIT = (
        d["T"], d["HB"], d["NIC"], d["ICW"], d["KB"], d["NKG"], d["KGW"],
        d["NHC"], d["ER"], d["ERB"], d["TK"], d["NIT"])
    H, I, E, RK, K, ICH, KGS = (d["H"], d["I"], d["E"], d["RK"], d["K"],
                                d["ICH"], d["KGS"])
    S_TOT = d["S_TOT"]

    Y.op("dma_start", out=t_["ident_sb"][:], in_=t_["ident"][:, :],
         inc=[(sm["sAux"], 16)])
    Y.op("dma_start", out=t_["onesr_sb"][:], in_=t_["onesr"][:, :],
         inc=[(sm["sAux"], 16)])
    Y.op("dma_start", out=t_["ones_col16"][:], in_=t_["onesc"][:, :],
         inc=[(sm["sAux"], 16)], mark=("aux",), w=w)
    G.op("dma_start", out=t_["keysT_sb"][:], in_=t_["keysT"][:, :],
         inc=[(sm["sAuxG"], 16)])
    G.op("dma_start", out=t_["wqT_sb"][:],
         in_=t_["wqT"].rearrange("(hb p) k -> p hb k", p=128),
         inc=[(sm["sAuxG"], 16)], mark=("auxg",), w=w)
    V.op("memset", t_["epsb"][:], 1e-5, inc=True)
    V.op("memset", t_["cm8"][:], -8.0, inc=True, mark=("cm8",), w=w)
    V.op("memset", t_["colsb"][0:1, :], 0.0, inc=True, mark=("vinit",), w=w)

    gci = [0]

    def a_stats(t):
        par = t % 2
        own = t >= T
        src_ = t_["x_own"] if own else t_["x_sib"]
        r0 = (t - T if own else t) * 128
        if t >= 2:
            Y.need(w, ("ln_sub", t - 2))
        Y.op("dma_start", out=t_["xA"][par][:], in_=src_[r0:r0 + 128, :],
             inc=[(sm["sXA"][par], 16)], mark=("xa", t), w=w)
        V.need(w, ("xa", t))
        V.chain()
        V.op("tensor_reduce", out=t_["msum"][par][:], in_=t_["xA"][par][:],
             axis=AX.X, op=ALU.add, inc=True)
        V.chain()
        V.op("tensor_scalar", out=t_["mrow"][par][:], in0=t_["msum"][par][:],
             scalar1=1.0 / H, scalar2=None, op0=ALU.mult, inc=True)
        V.chain()
        if t >= 2:
            V.need(w, ("ln_sqr", t - 2))
        V.op("tensor_scalar", out=t_["xc"][par][:], in0=t_["xA"][par][:],
             scalar1=t_["mrow"][par][:], scalar2=None, op0=ALU.subtract,
             inc=True, mark=("ln_sub", t), w=w)
        S.need(w, ("ln_sub", t))
        S.chain()
        S.op("activation", out=t_["scr4k"][:], in_=t_["xc"][par][:],
             func=AF.Square, accum_out=t_["ssq"][par][:], inc=True,
             mark=("ln_sqr", t), w=w)
        S.chain()
        S.need(w, ("vinit",))
        if t >= 2:
            S.need(w, ("ln_rstd", t - 2))
        S.op("activation", out=t_["sq"][par][:], in_=t_["ssq"][par][:],
             func=AF.Sqrt, scale=1.0 / H, bias=t_["epsb"][:], inc=True,
             mark=("ln_sq", t), w=w)

    def a_finish(t):
        par = t % 2
        own = t >= T
        to = t - T
        V.need(w, ("ln_sq", t))
        V.chain()
        V.op("reciprocal", out=t_["rstd"][par][:], in_=t_["sq"][par][:],
             inc=True, mark=("ln_rstd", t), w=w)
        V.chain()
        xnbuf = t_["xn16"][par]
        if t >= 2:
            V.need(w, ("xn_done", t - 2))
        V.op("tensor_scalar", out=xnbuf[:], in0=t_["xc"][par][:],
             scalar1=t_["rstd"][par][:], scalar2=None, op0=ALU.mult,
             inc=True, mark=("xn", t), w=w)
        P.need(w, ("xn", t))
        P.need(w, ("aux",))
        if own:
            ng = (HB + 7) // 8
            for g in range(ng):
                gw = min(8, HB - g * 8)
                gg = to * ng + g
                pst = t_["pstr"][gg % 2]
                if gg >= 2:
                    P.need(w, ("xnr_evac", gg - 2))
                for k in range(gw):
                    hb = g * 8 + k
                    P.op("transpose", pst[:, k, :],
                         xnbuf[:, hb * 128:(hb + 1) * 128], t_["ident_sb"][:],
                         inc=(True if k == gw - 1 else None),
                         mark=(("trp", gg) if k == gw - 1 else None), w=w)
                S.need(w, ("trp", gg))
                S.op("copy", out=t_["xnT"][:, g * 8:g * 8 + gw,
                                           to * 128:(to + 1) * 128],
                     in_=pst[:, 0:gw, :], inc=True, mark=("xnr_evac", gg), w=w)
        for j in range(NHC):
            gc = gci[0]
            cps = t_["colps"][gc % 2]
            if gc >= 2:
                P.need(w, ("col_acc", gc - 2))
            P.op("matmul", cps[0:1, :], t_["ones_col16"][:, 0:1],
                 xnbuf[:, j * 512:(j + 1) * 512], start=True, stop=True,
                 inc=True, mark=("col_mm", gc), w=w)
            V.need(w, ("col_mm", gc))
            V.chain()
            V.op("tensor_tensor", out=t_["colsb"][0:1, j * 512:(j + 1) * 512],
                 in0=t_["colsb"][0:1, j * 512:(j + 1) * 512],
                 in1=cps[0:1, :], op=ALU.add, inc=True,
                 mark=("col_acc", gc), w=w)
            gci[0] += 1
        w.prod[("xn_done", t)] = w.prod[("col_mm", gci[0] - 1)]

    for t in range(2 * T + 1):
        if t < 2 * T:
            a_stats(t)
        if t >= 1:
            a_finish(t - 1)

    V.chain()
    V.op("tensor_scalar", out=t_["mean16"][0:1, :], in0=t_["colsb"][0:1, :],
         scalar1=1.0 / S_TOT, scalar2=None, op0=ALU.mult, inc=True,
         mark=("mean16",), w=w)
    P.need(w, ("mean16",))
    P.need(w, ("col_acc", gci[0] - 1))
    for hb in range(HB):
        P.op("transpose", t_["psq"][:, hb, 0:1],
             t_["mean16"][0:1, hb * 128:(hb + 1) * 128],
             t_["ident_sb"][0:1, 0:1],
             inc=(True if hb == HB - 1 else None),
             mark=(("meanT_mm",) if hb == HB - 1 else None), w=w)
    S.need(w, ("meanT_mm",))
    S.op("copy", out=t_["meanT16"][:, :], in_=t_["psq"][:, :, 0], inc=True,
         mark=("meanT16",), w=w)
    P.need(w, ("meanT16",))
    P.need(w, ("auxg",))
    for hb in range(HB):
        P.op("matmul", t_["qps"][0:1, :], t_["meanT16"][:, hb:hb + 1],
             t_["wqT_sb"][:, hb, :], start=(hb == 0), stop=(hb == HB - 1),
             inc=(True if hb == HB - 1 else None),
             mark=(("q_mm",) if hb == HB - 1 else None), w=w)
    S.need(w, ("q_mm",))
    S.op("copy", out=t_["q16"][0:1, :], in_=t_["qps"][0:1, :], inc=True,
         mark=("q16",), w=w)
    P.need(w, ("q16",))
    P.op("transpose", t_["psq2"][:, 0:1], t_["q16"][0:1, :],
         t_["ident_sb"][0:1, 0:1], inc=True, mark=("qT_mm",), w=w)
    S.need(w, ("qT_mm",))
    S.op("copy", out=t_["q16T"][:, 0:1], in_=t_["psq2"][:, 0:1], inc=True,
         mark=("q16T",), w=w)
    P.need(w, ("q16T",))
    P.op("matmul", t_["scps"][0:1, :], t_["q16T"][:, 0:1], t_["keysT_sb"][:, :],
         start=True, stop=True, inc=True, mark=("sc_mm",), w=w)
    S.need(w, ("sc_mm",))
    S.op("mul", out=t_["scores"][0:1, :], in_=t_["scps"][0:1, :],
         mul=1.0 / math.sqrt(K), inc=True, mark=("scores",), w=w)
    V.need(w, ("scores",))
    V.chain()
    V.op("tensor_reduce", out=t_["smax"][0:1, :], in_=t_["scores"][0:1, :],
         axis=AX.X, op=ALU.max, inc=True)
    V.chain()
    V.op("tensor_scalar", out=t_["nsmax"][0:1, :], in0=t_["smax"][0:1, :],
         scalar1=-1.0, scalar2=None, op0=ALU.mult, inc=True,
         mark=("nsmax",), w=w)
    S.need(w, ("nsmax",))
    S.chain()
    S.op("activation", out=t_["esc"][0:1, :], in_=t_["scores"][0:1, :],
         func=AF.Exp, bias=t_["nsmax"][0:1, 0:1], inc=True,
         mark=("esc",), w=w)
    V.need(w, ("esc",))
    V.chain()
    V.op("tensor_reduce", out=t_["ssum1"][0:1, :], in_=t_["esc"][0:1, :],
         axis=AX.X, op=ALU.add, inc=True)
    V.chain()
    V.op("reciprocal", out=t_["rsum"][0:1, :], in_=t_["ssum1"][0:1, :],
         inc=True)
    V.chain()
    V.op("tensor_scalar", out=t_["coef16"][0:1, :], in0=t_["esc"][0:1, :],
         scalar1=t_["rsum"][0:1, 0:1], scalar2=None, op0=ALU.mult, inc=True,
         mark=("coef16",), w=w)
    P.need(w, ("coef16",))
    P.op("matmul", t_["cbps"][:, :], t_["onesr_sb"][0:1, :],
         t_["coef16"][0:1, :], start=True, stop=True, inc=True,
         mark=("cb_mm",), w=w)
    V.need(w, ("cb_mm",))
    V.chain()
    V.op("tensor_copy", out=t_["coefbc"][:, :], in_=t_["cbps"][:, :], inc=True)
    V.chain()
    for eb in range(ERB):
        V.op("tensor_copy", out=t_["coefEB"][0:64, eb:eb + 1],
             in_=t_["coefbc"][0:64, 2 * eb:2 * eb + 1], inc=True)
        V.op("tensor_copy", out=t_["coefEB"][64:128, eb:eb + 1],
             in_=t_["coefbc"][64:128, 2 * eb + 1:2 * eb + 2],
             inc=(True if eb == ERB - 1 else None),
             mark=(("coefEB",) if eb == ERB - 1 else None), w=w)

    def b_phase(which):
        wname = "wgT" if which == 1 else "wuT"
        wsem = sm["sWG"] if which == 1 else sm["sWU"]
        base = 0 if which == 1 else NIC
        for c in range(NIC):
            fc = base + c
            par = fc % 2
            cw = ICW[c]
            c0 = c * ICH
            if fc < 2:
                G.need(w, ("coefEB",))
                G.need(w, ("esc",))
                G.need(w, ("cb_mm",))
            else:
                G.need(w, ("grp", fc - 2, T - 1))
            G.op("dma_start", out=t_["wch"][par][:, :, 0:cw],
                 in_=t_[wname].rearrange("(hb p) i -> p hb i", p=128)[:, :, c0:c0 + cw],
                 inc=[(wsem[par], 16)], mark=("wld", fc), w=w)
            for t in range(T):
                g = fc * T + t
                bank = t_["psB"][g % 4]
                P.need(w, ("wld", fc))
                if which == 1:
                    P.need(w, ("xnr_evac", (t + 1) * ((HB + 7) // 8) - 1))
                    if fc == 0:
                        P.need(w, ("coefEB",))
                        P.need(w, ("esc",))
                if g >= 4:
                    P.need(w, ("ev", g - 4))
                for hb in range(HB):
                    P.op("matmul", bank[:, 0:cw],
                         t_["xnT"][:, hb, t * 128:(t + 1) * 128],
                         t_["wch"][par][:, hb, 0:cw],
                         start=(hb == 0), stop=(hb == HB - 1),
                         inc=(True if hb == HB - 1 else None),
                         mark=(("grp", fc, t) if hb == HB - 1 else None),
                         w=w)
                S.need(w, ("grp", fc, t))
                if fc >= 2:
                    S.need(w, ("spl", fc - 2))
                    if which == 1:
                        S.need(w, ("spl_a", c - 2))
                S.op("activation" if which == 1 else "copy",
                     **(dict(out=t_["sstage"][par][:, t, 0:cw],
                             in_=bank[:, 0:cw], func=AF.Silu) if which == 1
                        else dict(out=t_["sstage"][par][:, t, 0:cw],
                                  in_=bank[:, 0:cw])),
                     inc=True)
                if which == 1:
                    S.chain()
                    S.op("activation", out=t_["astage"][par][:, t, 0:cw],
                         in_=t_["sstage"][par][:, t, 0:cw], func=AF.Abs,
                         inc=True, mark=("ev", g), w=w)
                else:
                    w.mark_here(("ev", g), "cS")
                    if t % 2 == 0 or t == 7:
                        s_steps(1)
            Y.need(w, ("ev", fc * T + T - 1))
            if which == 1:
                Y.op("dma_start",
                     out=t_["sp_s"].rearrange("t p i -> p t i")[:, :, c0:c0 + cw],
                     in_=t_["sstage"][par][:, :, 0:cw],
                     inc=[(sm["sSpS"][par], 16)], mark=("spl", fc), w=w)
                Y.op("dma_start",
                     out=t_["sp_a"].rearrange("t p i -> p t i")[:, :, c0:c0 + cw],
                     in_=t_["astage"][par][:, :, 0:cw],
                     inc=[(sm["sSpA"][par], 16)], mark=("spl_a", c), w=w)
            else:
                Y.op("dma_start",
                     out=t_["sp_u"].rearrange("t p i -> p t i")[:, :, c0:c0 + cw],
                     in_=t_["sstage"][par][:, :, 0:cw],
                     inc=[(sm["sSpU"][par], 16)], mark=("spl", fc), w=w)
                w.prod[("spl_u", c)] = w.prod[("spl", fc)]
            if which == 2:
                v_steps(5)

    hw0 = 16.0
    vtiles = [t for t in range(T) if t % 2 == 0]
    stiles = [t for t in range(T) if t % 2 == 1]
    vst = {"i": 0, "it": -1, "prev": None}
    sst = {"i": 0, "it": -1, "prev": None}
    pstate = {"pieced": 0}

    def v_steps(n):
        while n > 0 and vst["i"] < len(vtiles):
            tile = vtiles[vst["i"]]
            if vst["it"] < 0:
                if vst["prev"] is None:
                    Y.need(w, ("spl_a", NIC - 1))
                    Y.need(w, ("spl_a", NIC - 2))
                else:
                    Y.need(w, ("cnt_done", vst["prev"]))
                Y.op("dma_start", out=t_["a16b"][:],
                     in_=t_["sp_a"][tile, :, :],
                     inc=[(sm["sAR"], 16)], mark=("a16", tile), w=w)
                vst["prev"] = tile
                V.chain()
                V.op("memset", t_["lob"][0][:], 0.0, inc=True)
                vst["it"] = 0
            k0 = vst["it"]
            k1 = min(k0 + n, NIT)
            for k in range(k0, k1):
                q, nq = k % 2, 1 - (k % 2)
                hw = hw0 * (0.5 ** (k + 1))
                V.need(w, ("a16", tile))
                V.chain()
                V.op("tensor_scalar", out=t_["tb"][nq][:], in0=t_["lob"][q][:],
                     scalar1=hw, scalar2=None, op0=ALU.add, inc=True)
                V.chain()
                V.op("tensor_scalar", out=t_["scrV"][:], in0=t_["a16b"][:],
                     scalar1=t_["tb"][nq][:], scalar2=None,
                     op0=ALU.is_ge, op1=ALU.add, accum_out=t_["cb"][:],
                     inc=True)
                V.chain()
                V.op("tensor_scalar", out=t_["geb"][:], in0=t_["cb"][:],
                     scalar1=float(TK), scalar2=None, op0=ALU.is_ge, inc=True)
                V.chain()
                V.op("scalar_tensor_tensor", out=t_["lob"][nq][:],
                     in0=t_["geb"][:], scalar=hw, in1=t_["lob"][q][:],
                     op0=ALU.mult, op1=ALU.add, inc=True)
            n -= k1 - k0
            vst["it"] = k1
            if vst["it"] == NIT:
                V.chain()
                V.op("tensor_copy", out=t_["tfinT"][:, tile:tile + 1],
                     in_=t_["lob"][NIT % 2][:], inc=True,
                     mark=("cnt_done", tile), w=w)
                vst["i"] += 1
                vst["it"] = -1

    def s_steps(n):
        while n > 0 and sst["i"] < len(stiles):
            tile = stiles[sst["i"]]
            if sst["it"] < 0:
                if sst["prev"] is None:
                    Y.need(w, ("spl_a", NIC - 1))
                    Y.need(w, ("spl_a", NIC - 2))
                else:
                    Y.need(w, ("cnt_done", sst["prev"]))
                Y.op("dma_start", out=t_["a16c"][:],
                     in_=t_["sp_a"][tile, :, :],
                     inc=[(sm["sAR2"], 16)], mark=("a16c", tile), w=w)
                sst["prev"] = tile
                S.need(w, ("cm8",))
                S.chain()
                S.op("copy", out=t_["negt"][0][:], in_=t_["cm8"][:], inc=True)
                S.chain()
                S.op("copy", out=t_["nh"][0][:], in_=t_["cm8"][:], inc=True)
                sst["it"] = 0
            k0 = sst["it"]
            k1 = min(k0 + n, NIT)
            for k in range(k0, k1):
                q, nq = k % 2, 1 - (k % 2)
                S.need(w, ("a16c", tile))
                S.chain()
                S.op("activation", out=t_["nh"][nq][:], in_=t_["nh"][q][:],
                     func=AF.Identity, scale=0.5, inc=True)
                S.chain()
                S.op("activation", out=t_["scrS"][:], in_=t_["a16c"][:, 0:I // 2],
                     func=AF.Sign, bias=t_["negt"][q][:],
                     accum_out=t_["Ssum2"][:], inc=True)
                S.chain()
                S.op("activation", out=t_["scrS"][:], in_=t_["a16c"][:, I // 2:I],
                     func=AF.Sign, bias=t_["negt"][q][:],
                     accum_out=t_["Ssum"][:], inc=True)
                S.chain()
                S.op("activation", out=t_["sgb"][:], in_=t_["Ssum2"][:],
                     func=AF.Sign, bias=t_["Ssum"][:], inc=True)
                S.chain()
                S.op("activation", out=t_["negt"][nq][:], in_=t_["sgb"][:],
                     func=AF.Identity, scale=t_["nh"][nq][:],
                     bias=t_["negt"][q][:], inc=True)
            n -= k1 - k0
            sst["it"] = k1
            if sst["it"] == NIT:
                S.chain()
                S.op("activation", out=t_["tfinT"][:, tile:tile + 1],
                     in_=t_["negt"][NIT % 2][:], func=AF.Identity, scale=-1.0,
                     inc=True, mark=("cnt_done", tile), w=w)
                sst["i"] += 1
                sst["it"] = -1

    def barrier_cb(Eng_):
        Eng_.need(w, ("spl", 2 * NIC - 1))
        Eng_.need(w, ("spl", 2 * NIC - 2))
        Eng_.need(w, ("spl_a", NIC - 1))
        Eng_.need(w, ("spl_a", NIC - 2))
        Eng_.need(w, ("grp", 2 * NIC - 1, T - 1))
        Eng_.need(w, ("ev", 2 * NIC * T - 1))

    def emit_piece_work(t):
        PBL = d["PB"]
        for g in range(len(PBL)):
            pg = t * len(PBL) + g
            par = pg % 2
            nb = PBL[g]
            k0 = sum(PBL[:g]) * 128
            kw_ = nb * 128
            if pg >= 2:
                Y.need(w, ("pb", pg - 2))
            else:
                barrier_cb(Y)
            Y.need(w, ("cnt_done", t))
            Y.op("dma_start", out=t_["spc"][par][:, 0:kw_],
                 in_=t_["sp_s"][t, :, k0:k0 + kw_],
                 inc=[(sm["sPP"][par], 16)])
            Y.op("dma_start", out=t_["upc"][par][:, 0:kw_],
                 in_=t_["sp_u"][t, :, k0:k0 + kw_],
                 inc=[(sm["sPP"][par], 16)], mark=("ppl", pg), w=w)
            V.need(w, ("ppl", pg))
            V.need(w, ("cnt_done", t))
            V.chain()
            V.op("scalar_tensor_tensor", out=t_["abs16"][:, 0:kw_],
                 in0=t_["spc"][par][:, 0:kw_], scalar=-1.0,
                 in1=t_["spc"][par][:, 0:kw_],
                 op0=ALU.mult, op1=ALU.max, inc=True)
            V.chain()
            V.op("scalar_tensor_tensor", out=t_["ms16"][:, 0:kw_],
                 in0=t_["abs16"][:, 0:kw_],
                 scalar=t_["tfinT"][:, t:t + 1],
                 in1=t_["spc"][par][:, 0:kw_],
                 op0=ALU.is_ge, op1=ALU.mult, inc=True)
            V.chain()
            if pg >= 2:
                V.need(w, ("ptTp", pg - 2))
            V.op("tensor_tensor", out=t_["p16"][par][:, 0:kw_],
                 in0=t_["ms16"][:, 0:kw_], in1=t_["upc"][par][:, 0:kw_],
                 op=ALU.mult, inc=True, mark=("pb", pg), w=w)
            P.need(w, ("pb", pg))
            nsg = (nb + 7) // 8
            for sg in range(nsg):
                gsg = gsgctr[0]
                gsgctr[0] += 1
                sgw = min(8, nb - sg * 8)
                pst = t_["pstr"][gsg % 2]
                if gsg >= 2:
                    P.need(w, ("ptev", gsg - 2))
                for k in range(sgw):
                    kb_ = sg * 8 + k
                    P.op("transpose", pst[:, k, :],
                         t_["p16"][par][:, kb_ * 128:(kb_ + 1) * 128],
                         t_["ident_sb"][:],
                         inc=(True if k == sgw - 1 else None),
                         mark=(("ptT", gsg) if k == sgw - 1 else None), w=w)
                if sg == nsg - 1:
                    w.prod[("ptTp", pg)] = w.prod[("ptT", gsg)]
                S.need(w, ("ptT", gsg))
                if pg >= 2:
                    S.need(w, ("pt_spl", pg - 2))
                S.op("copy", out=t_["ptstage"][par][:, sg * 8:sg * 8 + sgw, :],
                     in_=pst[:, 0:sgw, :], inc=True,
                     mark=(("ptev", gsg)), w=w)
            Y.need(w, ("ptev", gsgctr[0] - 1))
            Y.op("dma_start",
                 out=t_["sp_pt"][t, :, k0:k0 + kw_],
                 in_=t_["ptstage"][par][:, 0:nb, :],
                 inc=[(sm["sPTW"][par], 16)], mark=("pt_spl", pg), w=w)

    gsgctr = [0]

    b_phase(1)
    b_phase(2)
    while (vst["i"] < len(vtiles) or sst["i"] < len(stiles)
           or pstate["pieced"] < T):
        v_steps(NIT)
        s_steps(NIT)
        while (pstate["pieced"] < T
               and ("cnt_done", pstate["pieced"]) in w.prod):
            emit_piece_work(pstate["pieced"])
            pstate["pieced"] += 1

    for eb in range(ERB):
        par = 0
        if eb >= 1:
            G.need(w, ("hT_mm", eb - 1, T - 1))
        else:
            barrier_cb(G)
        G.op("dma_start", out=t_["ldc"][par][:, :, :],
             in_=t_["ldT"].rearrange("(hb p) r -> p hb r", p=128)[:, :, eb * 128:(eb + 1) * 128],
             inc=[(sm["sLD"][par], 16)], mark=("ld", eb), w=w)
        for t in range(T):
            P.need(w, ("ld", eb))
            if eb == 0 and t == 0:
                P.need(w, ("ev", 2 * NIC * T - 1))
            if eb > 0 or t > 0:
                P.need(w, ("hT_ev", eb - 1 if t == 0 else eb,
                           T - 1 if t == 0 else t - 1))
            for hb in range(HB):
                P.op("matmul", t_["psB"][0][:, 0:128],
                     t_["ldc"][par][:, hb, :],
                     t_["xnT"][:, hb, t * 128:(t + 1) * 128],
                     start=(hb == 0), stop=(hb == HB - 1),
                     inc=(True if hb == HB - 1 else None),
                     mark=(("hT_mm", eb, t) if hb == HB - 1 else None), w=w)
            S.need(w, ("hT_mm", eb, t))
            S.need(w, ("coefEB",))
            S.op("activation", out=t_["hpT16"][:, t, eb, :],
                 in_=t_["psB"][0][:, 0:128], func=AF.Identity,
                 scale=t_["coefEB"][:, eb:eb + 1], inc=True,
                 mark=("hT_ev", eb, t), w=w)
    for t in range(T):
        w.prod[("hpT", t)] = w.prod[("hT_ev", ERB - 1, t)]

    NPP = len(d["PB"])

    def barrier_c2(Eng_):
        Eng_.need(w, ("ptev", gsgctr[0] - 1))
        Eng_.need(w, ("ptev", gsgctr[0] - 2))
        Eng_.need(w, ("pb", T * NPP - 1))
        for i in range(2):
            Eng_.need(w, ("pt_spl", T * NPP - 1 - i))
            Eng_.need(w, ("ppl", T * NPP - 1 - i))
        Eng_.need(w, ("hT_mm", ERB - 1, T - 1))
        Eng_.need(w, ("hT_ev", ERB - 1, T - 1))
        Eng_.need(w, ("ld", ERB - 1))

    for hc in range(NHC):
        h0 = hc * 512
        for half in range(2):
            gidx = hc * 2 + half
            hb0 = half * (HB // 2)
            if gidx >= 2:
                G.need(w, ("c2base", hc - 1, T - 1))
            else:
                barrier_c2(G)
            G.op("dma_start", out=t_["woc"][gidx % 2][:, :, :],
                 in_=t_["woT"].rearrange("(hb p) o -> p hb o", p=128)[:, hb0:hb0 + HB // 2,
                                                                     h0:h0 + 512],
                 inc=[(sm["sWO"][gidx % 2], 16)], mark=("wo", hc, half), w=w)
        if hc >= 2:
            G.need(w, ("c2delta", hc - 2, T - 1))
        G.op("dma_start", out=t_["luc"][hc % 2][:, :, :],
             in_=t_["luT"].rearrange("(eb p) o -> p eb o", p=128)[:, :, h0:h0 + 512],
             inc=[(sm["sLU"][hc % 2], 16)], mark=("lu", hc), w=w)

        for t in range(T):
            bank = t_["psC"][t]
            if hc >= 1:
                P.need(w, ("c2ev", hc - 1, t))
            elif t >= 4:
                P.need(w, ("ptev", gsgctr[0] - 1))
            else:
                P.need(w, ("hT_ev", ERB - 1, T - 1))
                P.need(w, ("ev", 2 * NIC * T - 1))
            for hb in range(HB):
                half = hb // (HB // 2)
                P.need(w, ("wo", hc, half))
                P.op("matmul", bank[:, :],
                     t_["xnT"][:, hb, t * 128:(t + 1) * 128],
                     t_["woc"][(hc * 2 + half) % 2][:, hb - half * (HB // 2), :],
                     start=(hb == 0), stop=False,
                     inc=(True if hb == HB - 1 else None),
                     mark=(("c2base", hc, t) if hb == HB - 1 else None), w=w)
            P.need(w, ("lu", hc))
            P.need(w, ("hpT", t))
            for b in range(ERB):
                P.op("matmul", bank[:, :], t_["hpT16"][:, t, b, :],
                     t_["luc"][hc % 2][:, b, :], start=False, stop=False,
                     inc=(True if b == ERB - 1 else None),
                     mark=(("c2delta", hc, t) if b == ERB - 1 else None), w=w)
        for g in range(NKG):
            kg = KGW[g]
            seq = hc * NKG + g
            par = seq % 2
            if seq >= 2:
                Y.need(w, ("c2kg", seq - 2))
                G.need(w, ("c2kg", seq - 2))
            else:
                barrier_c2(Y)
            Y.op("dma_start", out=t_["ptc"][par][:, :, 0:kg, :],
                 in_=t_["sp_pt"].rearrange("t p kc -> p t kc")[:, :, g * KGS * 128:(g * KGS + kg) * 128]
                 .rearrange("p t (k c) -> p t k c", c=128),
                 inc=[(sm["sPTR"][par], 16)], mark=("ptr", hc, g), w=w)
            G.op("dma_start", out=t_["wdc"][par][:, 0:kg, :],
                 in_=t_["wdT"].rearrange("(k p) o -> p k o", p=128)[:, g * KGS:g * KGS + kg,
                                                                   h0:h0 + 512],
                 inc=[(sm["sWD"][par], 16)], mark=("wd", hc, g), w=w)
            P.need(w, ("ptr", hc, g))
            P.need(w, ("wd", hc, g))
            last_g = (g == NKG - 1)
            for k in range(kg):
                for t in range(T):
                    P.op("matmul", t_["psC"][t][:, :],
                         t_["ptc"][par][:, t, k, :], t_["wdc"][par][:, k, :],
                         start=False, stop=(last_g and k == kg - 1),
                         inc=(True if (k == kg - 1 and t == T - 1) else None),
                         mark=(("c2kg", seq) if (k == kg - 1 and t == T - 1)
                               else None), w=w)
        for t in range(T):
            par2 = (hc * T + t) % 2
            et = hc * T + t
            if et >= 2:
                Y.need(w, ("c2ev", hc, t - 2) if t >= 2 else
                       ("c2ev", hc - 1, T - 2 + t))
            else:
                barrier_c2(Y)
            Y.op("dma_start", out=t_["xstage"][par2][:, :],
                 in_=t_["x_own"][t * 128:(t + 1) * 128, h0:h0 + 512],
                 inc=[(sm["sXC"][par2], 16)], mark=("xc2", hc, t), w=w)
            V.need(w, ("c2kg", hc * NKG + NKG - 1))
            V.need(w, ("xc2", hc, t))
            if et >= 2:
                V.need(w, ("ow", hc, t - 2) if t >= 2 else
                       ("ow", hc - 1, T - 2 + t))
            V.chain()
            V.op("tensor_tensor", out=t_["ostage"][par2][:, :],
                 in0=t_["xstage"][par2][:, :], in1=t_["psC"][t][:, :],
                 op=ALU.add, inc=True, mark=("c2ev", hc, t), w=w)
            Y.need(w, ("c2ev", hc, t))
            Y.op("dma_start", out=t_["out"][t * 128:(t + 1) * 128, h0:h0 + 512],
                 in_=t_["ostage"][par2][:, :],
                 inc=[(sm["sOW"][par2], 16)], mark=("ow", hc, t), w=w)

    if d.get("DEBUG"):
        Y.need(w, ("cnt_done", T - 1))
        Y.op("dma_start", out=t_["dbg_thr"][:, :], in_=t_["tfinT"][:, :],
             inc=[(sm["sAR"], 16)])
        Y.wait(sm["sAR"], 16 * (T + 1))
    tot = NHC * T
    Y.wait(sm["sOW"][0], 16 * ((tot + 1) // 2))
    Y.wait(sm["sOW"][1], 16 * (tot // 2))


def build_nc(cfg):
    d = derive(cfg)
    T, HB, NKG, NHC, ERB = d["T"], d["HB"], d["NKG"], d["NHC"], d["ERB"]
    H, I, E, RK, K, ICH, KGS = (d["H"], d["I"], d["E"], d["RK"], d["K"],
                                d["ICH"], d["KGS"])
    ER = d["ER"]
    ROWS = d["ROWS"]

    nc = bass.Bass("TRN2", target_bir_lowering=False)
    t_ = {}
    t_["x_own"] = nc.dram_tensor("x_own", [ROWS, H], F32, kind="ExternalInput").ap()
    t_["x_sib"] = nc.dram_tensor("x_sib", [ROWS, H], F32, kind="ExternalInput").ap()
    t_["wgT"] = nc.dram_tensor("wgT", [H, I], F32, kind="ExternalInput").ap()
    t_["wuT"] = nc.dram_tensor("wuT", [H, I], F32, kind="ExternalInput").ap()
    t_["wdT"] = nc.dram_tensor("wdT", [I, H], F32, kind="ExternalInput").ap()
    t_["woT"] = nc.dram_tensor("woT", [H, H], F32, kind="ExternalInput").ap()
    t_["ldT"] = nc.dram_tensor("ldT", [H, ER], F32, kind="ExternalInput").ap()
    t_["luT"] = nc.dram_tensor("luT", [ER, H], F32, kind="ExternalInput").ap()
    t_["wqT"] = nc.dram_tensor("wqT", [H, K], F32, kind="ExternalInput").ap()
    t_["keysT"] = nc.dram_tensor("keysT", [K, E], F32, kind="ExternalInput").ap()
    t_["ident"] = nc.dram_tensor("ident", [128, 128], F16, kind="ExternalInput").ap()
    t_["onesr"] = nc.dram_tensor("onesr", [1, 128], F16, kind="ExternalInput").ap()
    t_["onesc"] = nc.dram_tensor("onesc", [128, 1], F16, kind="ExternalInput").ap()
    t_["out"] = nc.dram_tensor("out", [ROWS, H], F32, kind="ExternalOutput").ap()
    spk = "ExternalOutput" if cfg.get("DEBUG") else "Internal"
    t_["sp_s"] = nc.dram_tensor("sp_s", [T, 128, I], F16, kind=spk).ap()
    t_["sp_a"] = nc.dram_tensor("sp_a", [T, 128, I], F16, kind=spk).ap()
    t_["sp_u"] = nc.dram_tensor("sp_u", [T, 128, I], F16, kind=spk).ap()
    t_["sp_pt"] = nc.dram_tensor("sp_pt", [T, 128, d["KB"] * 128], F16,
                                 kind=spk).ap()
    if cfg.get("DEBUG"):
        t_["dbg_thr"] = nc.dram_tensor("dbg_thr", [128, T], F32,
                                       kind="ExternalOutput").ap()

    stack_pers = ExitStack()
    stack_A = ExitStack()
    stack_B = ExitStack()
    stack_C1a = ExitStack()
    stack_C1b = ExitStack()
    stack_C2 = ExitStack()
    ep = stack_pers.enter_context

    t_["xnT"] = ep(nc.sbuf_tensor("xnT", [128, HB, T * 128], F16))
    t_["ident_sb"] = ep(nc.sbuf_tensor("ident_sb", [128, 128], F16))
    t_["onesr_sb"] = ep(nc.sbuf_tensor("onesr_sb", [1, 128], F16))
    t_["ones_col16"] = ep(nc.sbuf_tensor("ones_col16", [128, 1], F16))
    t_["keysT_sb"] = ep(nc.sbuf_tensor("keysT_sb", [128, E], F16))
    t_["coefbc"] = ep(nc.sbuf_tensor("coefbc", [128, E], F32))
    t_["coefEB"] = ep(nc.sbuf_tensor("coefEB", [128, ERB], F32))
    t_["tfinT"] = ep(nc.sbuf_tensor("tfinT", [128, T], F32))
    t_["epsb"] = ep(nc.sbuf_tensor("epsb", [128, 1], F32))
    for nm in ["cb", "geb", "db", "ntS", "Ssum"]:
        t_[nm] = ep(nc.sbuf_tensor(nm, [128, 1], F32))
    for nm in ["msum", "mrow", "ssq", "sq", "rstd"]:
        t_[nm] = [ep(nc.sbuf_tensor(nm + str(i), [128, 1], F32))
                  for i in range(2)]
    t_["lob"] = [ep(nc.sbuf_tensor(f"lob{i}", [128, 1], F32)) for i in range(2)]
    t_["cm8"] = ep(nc.sbuf_tensor("cm8", [128, 1], F32))
    t_["negt"] = [ep(nc.sbuf_tensor(f"negt{i}", [128, 1], F32)) for i in range(2)]
    t_["nh"] = [ep(nc.sbuf_tensor(f"nh{i}", [128, 1], F32)) for i in range(2)]
    t_["sgb"] = ep(nc.sbuf_tensor("sgb", [128, 1], F32))
    t_["Ssum2"] = ep(nc.sbuf_tensor("Ssum2", [128, 1], F32))
    t_["tb"] = [ep(nc.sbuf_tensor(f"tbs{i}", [128, 1], F32)) for i in range(2)]
    for nm, shp in [("smax", [1, 1]), ("nsmax", [1, 1]), ("ssum1", [1, 1]),
                    ("rsum", [1, 1]), ("scores", [1, E]), ("esc", [1, E])]:
        t_[nm] = ep(nc.sbuf_tensor(nm, shp, F32))
    t_["coef16"] = ep(nc.sbuf_tensor("coef16", [1, E], F16))
    t_["q16"] = ep(nc.sbuf_tensor("q16", [1, K], F16))
    t_["q16T"] = ep(nc.sbuf_tensor("q16T", [128, 1], F16))
    t_["meanT16"] = ep(nc.sbuf_tensor("meanT16", [128, HB], F16))

    ea = stack_A.enter_context
    t_["xA"] = [ea(nc.sbuf_tensor(f"xA{i}", [128, H], F32)) for i in range(2)]
    t_["xc"] = [ea(nc.sbuf_tensor(f"xc{i}", [128, H], F32)) for i in range(2)]
    t_["scr4k"] = ea(nc.sbuf_tensor("scr4k", [128, H], F16))
    t_["xn16"] = [ea(nc.sbuf_tensor(f"xn16{i}", [128, H], F16)) for i in range(2)]
    t_["colsb"] = ea(nc.sbuf_tensor("colsb", [1, H], F32))
    t_["mean16"] = ea(nc.sbuf_tensor("mean16", [1, H], F16))
    t_["wqT_sb"] = ea(nc.sbuf_tensor("wqT_sb", [128, HB, K], F16))
    stack_A.close()

    eb_ = stack_B.enter_context
    t_["wch"] = [eb_(nc.sbuf_tensor(f"wch{i}", [128, HB, ICH], F16))
                 for i in range(2)]
    t_["sstage"] = [eb_(nc.sbuf_tensor(f"sstage{i}", [128, T, ICH], F16))
                    for i in range(2)]
    stack_B1x = ExitStack()
    t_["astage"] = [stack_B1x.enter_context(
        nc.sbuf_tensor(f"astage{i}", [128, T, ICH], F16)) for i in range(2)]
    stack_B1x.close()

    ec = stack_C1a.enter_context
    t_["a16b"] = ec(nc.sbuf_tensor("a16b", [128, I], F16, side="right"))
    t_["scrV"] = ec(nc.sbuf_tensor("scrV", [128, I], U8, side="right"))
    t_["scrS"] = ec(nc.sbuf_tensor("scrS", [128, I // 2], U8, side="right"))
    t_["a16c"] = ec(nc.sbuf_tensor("a16c", [128, I], F16, side="right"))

    stack_B.close()

    stack_hp = ExitStack()
    t_["hpT16"] = stack_hp.enter_context(
        nc.sbuf_tensor("hpT16", [128, T, ERB, 128], F16))


    e1 = stack_C1b.enter_context
    pcw = d["PBMAX"] * 128
    t_["spc"] = [e1(nc.sbuf_tensor(f"spc{i}", [128, pcw], F16)) for i in range(2)]
    t_["abs16"] = e1(nc.sbuf_tensor("abs16", [128, pcw], F16))
    t_["upc"] = [e1(nc.sbuf_tensor(f"upc{i}", [128, pcw], F16)) for i in range(2)]
    t_["ms16"] = e1(nc.sbuf_tensor("ms16", [128, pcw], F16))
    t_["p16"] = [e1(nc.sbuf_tensor(f"p16{i}", [128, pcw], F16)) for i in range(2)]
    t_["ptstage"] = [e1(nc.sbuf_tensor(f"ptstage{i}", [128, d["PBMAX"], 128], F16))
                     for i in range(2)]
    t_["ldc"] = [e1(nc.sbuf_tensor("ldc0", [128, HB, 128], F16))]
    stack_C1b.close()
    stack_C1a.close()

    e2 = stack_C2.enter_context
    t_["woc"] = [e2(nc.sbuf_tensor(f"woc{i}", [128, HB // 2, 512], F16))
                 for i in range(2)]
    t_["luc"] = [e2(nc.sbuf_tensor(f"luc{i}", [128, ERB, 512], F16))
                 for i in range(2)]
    t_["wdc"] = [e2(nc.sbuf_tensor(f"wdc{i}", [128, KGS, 512], F16))
                 for i in range(2)]
    t_["ptc"] = [e2(nc.sbuf_tensor(f"ptc{i}", [128, T, KGS, 128], F16))
                 for i in range(2)]
    t_["xstage"] = [e2(nc.sbuf_tensor(f"xstage{i}", [128, 512], F32))
                    for i in range(2)]
    t_["ostage"] = [e2(nc.sbuf_tensor(f"ostage{i}", [128, 512], F32))
                    for i in range(2)]
    stack_C2.close()

    ps_A = ExitStack()
    ps_pstr = ExitStack()
    ps_B = ExitStack()
    ps_C47 = ExitStack()
    t_["pstr"] = [ps_pstr.enter_context(
        nc.psum_tensor(f"pstr{i}", [128, 8, 128], F16, side="right"))
        for i in range(2)]
    pa = ps_A.enter_context
    t_["colps"] = [pa(nc.psum_tensor(f"colps{i}", [1, 512], F32))
                   for i in range(2)]
    psMf16 = pa(nc.psum_tensor("psMf16", [128, HB, 2], F16))
    t_["psq"] = psMf16
    t_["psq2"] = psMf16[:, 0, 0:1]
    psMf32 = pa(nc.psum_tensor("psMf32", [128, K + 2 * E], F32))
    t_["qps"] = psMf32[:, 0:K]
    t_["scps"] = psMf32[:, K:K + E]
    t_["cbps"] = psMf32[:, K + E:K + 2 * E]
    ps_A.close()
    t_["psB"] = [ps_B.enter_context(nc.psum_tensor(f"psB{i}", [128, 512], F32))
                 for i in range(4)]
    ps_pstr.close()
    t_["psC"] = list(t_["psB"])
    if T > 4:
        t_["psC"] = list(t_["psB"]) + [ps_C47.enter_context(
            nc.psum_tensor(f"psC{i}", [128, 512], F32, side="right"))
            for i in range(T - 4)]
    else:
        t_["psC"] = t_["psB"][:T]

    scalar_sems = {}
    list_sems = {}
    with ExitStack() as sem_stack:
        es = sem_stack.enter_context
        for nm in ["cV", "cS", "cT", "cY", "cG", "sAux", "sAuxG", "sAR",
                   "sAR2"]:
            scalar_sems[nm] = es(nc.semaphore(nm))
        for nm in ["sXA", "sWG", "sWU", "sSpS", "sSpA", "sSpU",
                   "sPTR", "sWD", "sWO", "sLU", "sLD", "sXC", "sOW"]:
            list_sems[nm] = [es(nc.semaphore(nm + str(i))) for i in range(2)]
        list_sems["sPTW"] = [es(nc.semaphore("sPTW" + str(i)))
                             for i in range(derive(cfg)["PD"])]
        list_sems["sPP"] = [es(nc.semaphore("sPP" + str(i)))
                            for i in range(derive(cfg)["PD"])]

        with nc.Block() as block:
            def run(active, engines):
                wlk = Walk(scalar_sems, list_sems, active, engines)
                schedule(wlk, t_, d)

            @block.sync
            def _(sync):
                run("sync", {"sync": sync})

            @block.vector
            def _(vector):
                run("vector", {"vector": vector})

            @block.scalar
            def _(scalar):
                run("scalar", {"scalar": scalar})

            @block.tensor
            def _(tensor):
                run("tensor", {"tensor": tensor})

            @block.gpsimd
            def _(gpsimd):
                run("gpsimd", {"gpsimd": gpsimd})

    return nc


def _prep_weights(w_gate, w_up, w_down, W_org, lora_down, lora_up, keys, Wq):
    H = W_org.shape[1]
    wgT = np.ascontiguousarray(np.asarray(w_gate, np.float32).T)
    wuT = np.ascontiguousarray(np.asarray(w_up, np.float32).T)
    wdT = np.ascontiguousarray(np.asarray(w_down, np.float32).T)
    woT = np.ascontiguousarray(np.asarray(W_org, np.float32).T)
    ldT = np.ascontiguousarray(
        np.asarray(lora_down, np.float32).transpose(1, 0, 2).reshape(H, -1))
    luT = np.ascontiguousarray(np.asarray(lora_up, np.float32).reshape(-1, H))
    wqT = np.ascontiguousarray(np.asarray(Wq, np.float32).T)
    keysT = np.ascontiguousarray(np.asarray(keys, np.float32).T)
    return wgT, wuT, wdT, woT, ldT, luT, wqT, keysT


def _mk_map(x_own, x_sib, wgT, wuT, wdT, woT, ldT, luT, wqT, keysT):
    return {
        "x_own": np.ascontiguousarray(x_own, dtype=np.float32),
        "x_sib": np.ascontiguousarray(x_sib, dtype=np.float32),
        "wgT": wgT, "wuT": wuT, "wdT": wdT, "woT": woT,
        "ldT": ldT, "luT": luT, "wqT": wqT, "keysT": keysT,
        "ident": np.eye(128, dtype=np.float16),
        "onesr": np.ones((1, 128), dtype=np.float16),
        "onesc": np.ones((128, 1), dtype=np.float16),
    }


def kernel(L0_attn_x, L0_attn_w_gate, L0_attn_w_up, L0_attn_w_down,
           L0_ffn_W_org, L0_ffn_lora_down, L0_ffn_lora_up, L0_ffn_lora_keys,
           L0_ffn_Wq):
    cfg = FULL_CFG
    x = np.asarray(L0_attn_x, dtype=np.float32)
    B, Sq, H = x.shape
    wgT, wuT, wdT, woT, ldT, luT, wqT, keysT = _prep_weights(
        L0_attn_w_gate, L0_attn_w_up, L0_attn_w_down, L0_ffn_W_org,
        L0_ffn_lora_down, L0_ffn_lora_up, L0_ffn_lora_keys, L0_ffn_Wq)
    nc = build_nc(cfg)
    RW = cfg["ROWS"]
    in_maps = []
    for c in range(8):
        b, hh = c // 2, c % 2
        in_maps.append(_mk_map(x[b, hh * RW:(hh + 1) * RW, :],
                               x[b, (1 - hh) * RW:(2 - hh) * RW, :],
                               wgT, wuT, wdT, woT, ldT, luT, wqT, keysT))
    res = run_bass_kernel_spmd(nc, in_maps, core_ids=list(range(8)))
    out = np.empty((B, Sq, H), dtype=np.float32)
    for c in range(8):
        b, hh = c // 2, c % 2
        out[b, hh * RW:(hh + 1) * RW, :] = res.results[c]["out"]
    return out


# version 20
# speedup vs baseline: 1.0901x; 1.0901x over previous
import math
import numpy as np
import concourse.bass as bass
import concourse.mybir as mybir
from concourse.bass_utils import run_bass_kernel_spmd
from contextlib import ExitStack

F32 = mybir.dt.float32
F16 = mybir.dt.float16
U8 = mybir.dt.uint8
AF = mybir.ActivationFunctionType
ALU = mybir.AluOpType
AX = mybir.AxisListType

FULL_CFG = dict(ROWS=1024, H=4096, I=11008, E=8, RK=64, K=128, ICH=512,
                KGS=8, NIT=28, S_TOT=2048, KAPPA=5056, PACE=6, PD=2, KGS2=8)
TINY_CFG = dict(ROWS=256, H=512, I=1024, E=8, RK=64, K=128, ICH=256,
                KGS=4, NIT=24, S_TOT=512, KAPPA=384, PACE=5, PD=4)


def derive(cfg):
    d = dict(cfg)
    d["T"] = cfg["ROWS"] // 128
    d["HB"] = cfg["H"] // 128
    ich, I = cfg["ICH"], cfg["I"]
    d["NIC"] = (I + ich - 1) // ich
    d["ICW"] = [min(ich, I - c * ich) for c in range(d["NIC"])]
    d["KB"] = I // 128
    kgs = cfg["KGS"]
    d["NKG"] = (d["KB"] + kgs - 1) // kgs
    d["KGW"] = [min(kgs, d["KB"] - g * kgs) for g in range(d["NKG"])]
    d["NHC"] = cfg["H"] // 512
    d["ER"] = cfg["E"] * cfg["RK"]
    d["ERB"] = d["ER"] // 128
    d["TK"] = I // 2
    npp = max(1, (d["KB"] + 21) // 22)
    base = d["KB"] // npp
    rem = d["KB"] - base * npp
    d["PB"] = [base + (1 if i < rem else 0) for i in range(npp)]
    d["PBMAX"] = max(d["PB"])
    return d


class Sem:
    __slots__ = ("h", "n")

    def __init__(self, handle):
        self.h = handle
        self.n = 0


class Eng:
    def __init__(self, name, real, sem):
        self.name = name
        self.real = real
        self.sem = sem
        self.seen = {}

    def wait(self, sem, val):
        if val <= 0:
            return
        k = id(sem)
        if self.seen.get(k, -1) >= val:
            return
        self.seen[k] = val
        if self.real is not None:
            self.real.wait_ge(sem.h, val)

    def chain(self):
        self.wait(self.sem, self.sem.n)

    def op(self, method, *args, inc=None, mark=None, w=None, **kw):
        incs = []
        if inc is True:
            incs = [(self.sem, 1)]
        elif inc:
            incs = inc
        if self.real is not None:
            inst = getattr(self.real, method)(*args, **kw)
            for (s, k) in incs:
                inst.then_inc(s.h, k)
        for (s, k) in incs:
            s.n += k
        if mark is not None:
            s, _ = incs[-1]
            w.prod[mark] = (s, s.n)

    def need(self, w, key):
        s, v = w.prod[key]
        self.wait(s, v)


class Walk:
    def mark_here(self, key, semname):
        s = self.sems[semname]
        self.prod[key] = (s, s.n)

    def __init__(self, scalar_sems, list_sems, active, engines):
        self.prod = {}
        self.sems = {k: Sem(v) for k, v in scalar_sems.items()}
        for k, lst in list_sems.items():
            self.sems[k] = [Sem(x) for x in lst]
        self.E = {}
        own_of = {"vector": "cV", "scalar": "cS", "tensor": "cT",
                  "sync": "cY", "gpsimd": "cG"}
        for name in ["vector", "scalar", "tensor", "sync", "gpsimd"]:
            self.E[name] = Eng(name, engines.get(name), self.sems[own_of[name]])


def schedule(w, t_, d):
    V = w.E["vector"]; S = w.E["scalar"]; P = w.E["tensor"]
    Y = w.E["sync"]; G = w.E["gpsimd"]
    sm = w.sems
    T, HB, NIC, ICW, KB, NKG, KGW, NHC, ER, ERB, TK, NIT = (
        d["T"], d["HB"], d["NIC"], d["ICW"], d["KB"], d["NKG"], d["KGW"],
        d["NHC"], d["ER"], d["ERB"], d["TK"], d["NIT"])
    H, I, E, RK, K, ICH, KGS = (d["H"], d["I"], d["E"], d["RK"], d["K"],
                                d["ICH"], d["KGS"])
    S_TOT = d["S_TOT"]

    Y.op("dma_start", out=t_["ident_sb"][:], in_=t_["ident"][:, :],
         inc=[(sm["sAux"], 16)])
    Y.op("dma_start", out=t_["onesr_sb"][:], in_=t_["onesr"][:, :],
         inc=[(sm["sAux"], 16)])
    Y.op("dma_start", out=t_["ones_col16"][:], in_=t_["onesc"][:, :],
         inc=[(sm["sAux"], 16)], mark=("aux",), w=w)
    G.op("dma_start", out=t_["keysT_sb"][:], in_=t_["keysT"][:, :],
         inc=[(sm["sAuxG"], 16)])
    G.op("dma_start", out=t_["wqT_sb"][:],
         in_=t_["wqT"].rearrange("(hb p) k -> p hb k", p=128),
         inc=[(sm["sAuxG"], 16)], mark=("auxg",), w=w)
    V.op("memset", t_["epsb"][:], 1e-5, inc=True)
    V.op("memset", t_["cm8"][:], -8.0, inc=True, mark=("cm8",), w=w)
    V.op("memset", t_["colsb"][0:1, :], 0.0, inc=True, mark=("vinit",), w=w)

    gci = [0]

    def a_stats(t):
        par = t % 2
        own = t >= T
        src_ = t_["x_own"] if own else t_["x_sib"]
        r0 = (t - T if own else t) * 128
        if t >= 2:
            Y.need(w, ("ln_sub", t - 2))
        Y.op("dma_start", out=t_["xA"][par][:], in_=src_[r0:r0 + 128, :],
             inc=[(sm["sXA"][par], 16)], mark=("xa", t), w=w)
        V.need(w, ("xa", t))
        V.chain()
        V.op("tensor_reduce", out=t_["msum"][par][:], in_=t_["xA"][par][:],
             axis=AX.X, op=ALU.add, inc=True)
        V.chain()
        V.op("tensor_scalar", out=t_["mrow"][par][:], in0=t_["msum"][par][:],
             scalar1=1.0 / H, scalar2=None, op0=ALU.mult, inc=True)
        V.chain()
        if t >= 2:
            V.need(w, ("ln_sqr", t - 2))
        V.op("tensor_scalar", out=t_["xc"][par][:], in0=t_["xA"][par][:],
             scalar1=t_["mrow"][par][:], scalar2=None, op0=ALU.subtract,
             inc=True, mark=("ln_sub", t), w=w)
        S.need(w, ("ln_sub", t))
        S.chain()
        S.op("activation", out=t_["scr4k"][:], in_=t_["xc"][par][:],
             func=AF.Square, accum_out=t_["ssq"][par][:], inc=True,
             mark=("ln_sqr", t), w=w)
        S.chain()
        S.need(w, ("vinit",))
        if t >= 2:
            S.need(w, ("ln_rstd", t - 2))
        S.op("activation", out=t_["sq"][par][:], in_=t_["ssq"][par][:],
             func=AF.Sqrt, scale=1.0 / H, bias=t_["epsb"][:], inc=True,
             mark=("ln_sq", t), w=w)

    def a_finish(t):
        par = t % 2
        own = t >= T
        to = t - T
        V.need(w, ("ln_sq", t))
        V.chain()
        V.op("reciprocal", out=t_["rstd"][par][:], in_=t_["sq"][par][:],
             inc=True, mark=("ln_rstd", t), w=w)
        V.chain()
        xnbuf = t_["xn16"][par]
        if t >= 2:
            V.need(w, ("xn_done", t - 2))
        V.op("tensor_scalar", out=xnbuf[:], in0=t_["xc"][par][:],
             scalar1=t_["rstd"][par][:], scalar2=None, op0=ALU.mult,
             inc=True, mark=("xn", t), w=w)
        P.need(w, ("xn", t))
        P.need(w, ("aux",))
        if own:
            ng = (HB + 7) // 8
            for g in range(ng):
                gw = min(8, HB - g * 8)
                gg = to * ng + g
                pst = t_["pstr"][gg % 2]
                if gg >= 2:
                    P.need(w, ("xnr_evac", gg - 2))
                for k in range(gw):
                    hb = g * 8 + k
                    P.op("transpose", pst[:, k, :],
                         xnbuf[:, hb * 128:(hb + 1) * 128], t_["ident_sb"][:],
                         inc=(True if k == gw - 1 else None),
                         mark=(("trp", gg) if k == gw - 1 else None), w=w)
                S.need(w, ("trp", gg))
                S.op("copy", out=t_["xnT"][:, g * 8:g * 8 + gw,
                                           to * 128:(to + 1) * 128],
                     in_=pst[:, 0:gw, :], inc=True, mark=("xnr_evac", gg), w=w)
        for j in range(NHC):
            gc = gci[0]
            cps = t_["colps"][gc % 2]
            if gc >= 2:
                P.need(w, ("col_acc", gc - 2))
            P.op("matmul", cps[0:1, :], t_["ones_col16"][:, 0:1],
                 xnbuf[:, j * 512:(j + 1) * 512], start=True, stop=True,
                 inc=True, mark=("col_mm", gc), w=w)
            V.need(w, ("col_mm", gc))
            V.chain()
            V.op("tensor_tensor", out=t_["colsb"][0:1, j * 512:(j + 1) * 512],
                 in0=t_["colsb"][0:1, j * 512:(j + 1) * 512],
                 in1=cps[0:1, :], op=ALU.add, inc=True,
                 mark=("col_acc", gc), w=w)
            gci[0] += 1
        w.prod[("xn_done", t)] = w.prod[("col_mm", gci[0] - 1)]

    for t in range(2 * T + 1):
        if t < 2 * T:
            a_stats(t)
        if t >= 1:
            a_finish(t - 1)

    V.chain()
    V.op("tensor_scalar", out=t_["mean16"][0:1, :], in0=t_["colsb"][0:1, :],
         scalar1=1.0 / S_TOT, scalar2=None, op0=ALU.mult, inc=True,
         mark=("mean16",), w=w)
    P.need(w, ("mean16",))
    P.need(w, ("col_acc", gci[0] - 1))
    for hb in range(HB):
        P.op("transpose", t_["psq"][:, hb, 0:1],
             t_["mean16"][0:1, hb * 128:(hb + 1) * 128],
             t_["ident_sb"][0:1, 0:1],
             inc=(True if hb == HB - 1 else None),
             mark=(("meanT_mm",) if hb == HB - 1 else None), w=w)
    S.need(w, ("meanT_mm",))
    S.op("copy", out=t_["meanT16"][:, :], in_=t_["psq"][:, :, 0], inc=True,
         mark=("meanT16",), w=w)
    P.need(w, ("meanT16",))
    P.need(w, ("auxg",))
    for hb in range(HB):
        P.op("matmul", t_["qps"][0:1, :], t_["meanT16"][:, hb:hb + 1],
             t_["wqT_sb"][:, hb, :], start=(hb == 0), stop=(hb == HB - 1),
             inc=(True if hb == HB - 1 else None),
             mark=(("q_mm",) if hb == HB - 1 else None), w=w)
    S.need(w, ("q_mm",))
    S.op("copy", out=t_["q16"][0:1, :], in_=t_["qps"][0:1, :], inc=True,
         mark=("q16",), w=w)
    P.need(w, ("q16",))
    P.op("transpose", t_["psq2"][:, 0:1], t_["q16"][0:1, :],
         t_["ident_sb"][0:1, 0:1], inc=True, mark=("qT_mm",), w=w)
    S.need(w, ("qT_mm",))
    S.op("copy", out=t_["q16T"][:, 0:1], in_=t_["psq2"][:, 0:1], inc=True,
         mark=("q16T",), w=w)
    P.need(w, ("q16T",))
    P.op("matmul", t_["scps"][0:1, :], t_["q16T"][:, 0:1], t_["keysT_sb"][:, :],
         start=True, stop=True, inc=True, mark=("sc_mm",), w=w)
    S.need(w, ("sc_mm",))
    S.op("mul", out=t_["scores"][0:1, :], in_=t_["scps"][0:1, :],
         mul=1.0 / math.sqrt(K), inc=True, mark=("scores",), w=w)
    V.need(w, ("scores",))
    V.chain()
    V.op("tensor_reduce", out=t_["smax"][0:1, :], in_=t_["scores"][0:1, :],
         axis=AX.X, op=ALU.max, inc=True)
    V.chain()
    V.op("tensor_scalar", out=t_["nsmax"][0:1, :], in0=t_["smax"][0:1, :],
         scalar1=-1.0, scalar2=None, op0=ALU.mult, inc=True,
         mark=("nsmax",), w=w)
    S.need(w, ("nsmax",))
    S.chain()
    S.op("activation", out=t_["esc"][0:1, :], in_=t_["scores"][0:1, :],
         func=AF.Exp, bias=t_["nsmax"][0:1, 0:1], inc=True,
         mark=("esc",), w=w)
    V.need(w, ("esc",))
    V.chain()
    V.op("tensor_reduce", out=t_["ssum1"][0:1, :], in_=t_["esc"][0:1, :],
         axis=AX.X, op=ALU.add, inc=True)
    V.chain()
    V.op("reciprocal", out=t_["rsum"][0:1, :], in_=t_["ssum1"][0:1, :],
         inc=True)
    V.chain()
    V.op("tensor_scalar", out=t_["coef16"][0:1, :], in0=t_["esc"][0:1, :],
         scalar1=t_["rsum"][0:1, 0:1], scalar2=None, op0=ALU.mult, inc=True,
         mark=("coef16",), w=w)
    P.need(w, ("coef16",))
    P.op("matmul", t_["cbps"][:, :], t_["onesr_sb"][0:1, :],
         t_["coef16"][0:1, :], start=True, stop=True, inc=True,
         mark=("cb_mm",), w=w)
    V.need(w, ("cb_mm",))
    V.chain()
    V.op("tensor_copy", out=t_["coefbc"][:, :], in_=t_["cbps"][:, :], inc=True)
    V.chain()
    for eb in range(ERB):
        V.op("tensor_copy", out=t_["coefEB"][0:64, eb:eb + 1],
             in_=t_["coefbc"][0:64, 2 * eb:2 * eb + 1], inc=True)
        V.op("tensor_copy", out=t_["coefEB"][64:128, eb:eb + 1],
             in_=t_["coefbc"][64:128, 2 * eb + 1:2 * eb + 2],
             inc=(True if eb == ERB - 1 else None),
             mark=(("coefEB",) if eb == ERB - 1 else None), w=w)

    def b_phase(which):
        wname = "wgT" if which == 1 else "wuT"
        wsem = sm["sWG"] if which == 1 else sm["sWU"]
        base = 0 if which == 1 else NIC
        for c in range(NIC):
            fc = base + c
            par = fc % 2
            cw = ICW[c]
            c0 = c * ICH
            if fc < 2:
                G.need(w, ("coefEB",))
                G.need(w, ("esc",))
                G.need(w, ("cb_mm",))
            else:
                G.need(w, ("grp", fc - 2, T - 1))
            G.op("dma_start", out=t_["wch"][par][:, :, 0:cw],
                 in_=t_[wname].rearrange("(hb p) i -> p hb i", p=128)[:, :, c0:c0 + cw],
                 inc=[(wsem[par], 16)], mark=("wld", fc), w=w)
            for t in range(T):
                g = fc * T + t
                bank = t_["psB"][g % 4]
                P.need(w, ("wld", fc))
                if which == 1:
                    P.need(w, ("xnr_evac", (t + 1) * ((HB + 7) // 8) - 1))
                    if fc == 0:
                        P.need(w, ("coefEB",))
                        P.need(w, ("esc",))
                if g >= 4:
                    P.need(w, ("ev", g - 4))
                for hb in range(HB):
                    P.op("matmul", bank[:, 0:cw],
                         t_["xnT"][:, hb, t * 128:(t + 1) * 128],
                         t_["wch"][par][:, hb, 0:cw],
                         start=(hb == 0), stop=(hb == HB - 1),
                         inc=(True if hb == HB - 1 else None),
                         mark=(("grp", fc, t) if hb == HB - 1 else None),
                         w=w)
                S.need(w, ("grp", fc, t))
                if fc >= 2:
                    S.need(w, ("spl", fc - 2))
                    if which == 1:
                        S.need(w, ("spl_a", c - 2))
                S.op("activation" if which == 1 else "copy",
                     **(dict(out=t_["sstage"][par][:, t, 0:cw],
                             in_=bank[:, 0:cw], func=AF.Silu) if which == 1
                        else dict(out=t_["sstage"][par][:, t, 0:cw],
                                  in_=bank[:, 0:cw])),
                     inc=True)
                if which == 1:
                    S.chain()
                    S.op("activation", out=t_["astage"][par][:, t, 0:cw],
                         in_=t_["sstage"][par][:, t, 0:cw], func=AF.Abs,
                         inc=True, mark=("ev", g), w=w)
                else:
                    w.mark_here(("ev", g), "cS")
                    if t % 2 == 0 or t == 7:
                        s_steps(1)
            Y.need(w, ("ev", fc * T + T - 1))
            if which == 1:
                Y.op("dma_start",
                     out=t_["sp_s"].rearrange("t p i -> p t i")[:, :, c0:c0 + cw],
                     in_=t_["sstage"][par][:, :, 0:cw],
                     inc=[(sm["sSpS"][par], 16)], mark=("spl", fc), w=w)
                Y.op("dma_start",
                     out=t_["sp_a"].rearrange("t p i -> p t i")[:, :, c0:c0 + cw],
                     in_=t_["astage"][par][:, :, 0:cw],
                     inc=[(sm["sSpA"][par], 16)], mark=("spl_a", c), w=w)
            else:
                Y.op("dma_start",
                     out=t_["sp_u"].rearrange("t p i -> p t i")[:, :, c0:c0 + cw],
                     in_=t_["sstage"][par][:, :, 0:cw],
                     inc=[(sm["sSpU"][par], 16)], mark=("spl", fc), w=w)
                w.prod[("spl_u", c)] = w.prod[("spl", fc)]
            if which == 2:
                v_steps(5)

    hw0 = 16.0
    vtiles = [t for t in range(T) if t % 2 == 0]
    stiles = [t for t in range(T) if t % 2 == 1]
    vst = {"i": 0, "it": -1, "prev": None}
    sst = {"i": 0, "it": -1, "prev": None}

    def v_steps(n):
        while n > 0 and vst["i"] < len(vtiles):
            tile = vtiles[vst["i"]]
            if vst["it"] < 0:
                if vst["prev"] is None:
                    Y.need(w, ("spl_a", NIC - 1))
                    Y.need(w, ("spl_a", NIC - 2))
                else:
                    Y.need(w, ("cnt_done", vst["prev"]))
                Y.op("dma_start", out=t_["a16b"][:],
                     in_=t_["sp_a"][tile, :, :],
                     inc=[(sm["sAR"], 16)], mark=("a16", tile), w=w)
                vst["prev"] = tile
                V.chain()
                V.op("memset", t_["lob"][0][:], 0.0, inc=True)
                vst["it"] = 0
            k0 = vst["it"]
            k1 = min(k0 + n, NIT)
            for k in range(k0, k1):
                q, nq = k % 2, 1 - (k % 2)
                hw = hw0 * (0.5 ** (k + 1))
                V.need(w, ("a16", tile))
                V.chain()
                V.op("tensor_scalar", out=t_["tb"][nq][:], in0=t_["lob"][q][:],
                     scalar1=hw, scalar2=None, op0=ALU.add, inc=True)
                V.chain()
                V.op("tensor_scalar", out=t_["scrV"][:], in0=t_["a16b"][:],
                     scalar1=t_["tb"][nq][:], scalar2=None,
                     op0=ALU.is_ge, op1=ALU.add, accum_out=t_["cb"][:],
                     inc=True)
                V.chain()
                V.op("tensor_scalar", out=t_["geb"][:], in0=t_["cb"][:],
                     scalar1=float(TK), scalar2=None, op0=ALU.is_ge, inc=True)
                V.chain()
                V.op("scalar_tensor_tensor", out=t_["lob"][nq][:],
                     in0=t_["geb"][:], scalar=hw, in1=t_["lob"][q][:],
                     op0=ALU.mult, op1=ALU.add, inc=True)
            n -= k1 - k0
            vst["it"] = k1
            if vst["it"] == NIT:
                V.chain()
                V.op("tensor_copy", out=t_["tfinT"][:, tile:tile + 1],
                     in_=t_["lob"][NIT % 2][:], inc=True,
                     mark=("cnt_done", tile), w=w)
                vst["i"] += 1
                vst["it"] = -1

    def s_steps(n):
        while n > 0 and sst["i"] < len(stiles):
            tile = stiles[sst["i"]]
            if sst["it"] < 0:
                if sst["prev"] is None:
                    Y.need(w, ("spl_a", NIC - 1))
                    Y.need(w, ("spl_a", NIC - 2))
                else:
                    Y.need(w, ("cnt_done", sst["prev"]))
                Y.op("dma_start", out=t_["a16c"][:],
                     in_=t_["sp_a"][tile, :, :],
                     inc=[(sm["sAR2"], 16)], mark=("a16c", tile), w=w)
                sst["prev"] = tile
                S.need(w, ("cm8",))
                S.chain()
                S.op("copy", out=t_["negt"][0][:], in_=t_["cm8"][:], inc=True)
                S.chain()
                S.op("copy", out=t_["nh"][0][:], in_=t_["cm8"][:], inc=True)
                sst["it"] = 0
            k0 = sst["it"]
            k1 = min(k0 + n, NIT)
            for k in range(k0, k1):
                q, nq = k % 2, 1 - (k % 2)
                S.need(w, ("a16c", tile))
                S.chain()
                S.op("activation", out=t_["nh"][nq][:], in_=t_["nh"][q][:],
                     func=AF.Identity, scale=0.5, inc=True)
                S.chain()
                S.op("activation", out=t_["scrS"][:], in_=t_["a16c"][:, 0:I // 2],
                     func=AF.Sign, bias=t_["negt"][q][:],
                     accum_out=t_["Ssum2"][:], inc=True)
                S.chain()
                S.op("activation", out=t_["scrS"][:], in_=t_["a16c"][:, I // 2:I],
                     func=AF.Sign, bias=t_["negt"][q][:],
                     accum_out=t_["Ssum"][:], inc=True)
                S.chain()
                S.op("activation", out=t_["sgb"][:], in_=t_["Ssum2"][:],
                     func=AF.Sign, bias=t_["Ssum"][:], inc=True)
                S.chain()
                S.op("activation", out=t_["negt"][nq][:], in_=t_["sgb"][:],
                     func=AF.Identity, scale=t_["nh"][nq][:],
                     bias=t_["negt"][q][:], inc=True)
            n -= k1 - k0
            sst["it"] = k1
            if sst["it"] == NIT:
                S.chain()
                S.op("activation", out=t_["tfinT"][:, tile:tile + 1],
                     in_=t_["negt"][NIT % 2][:], func=AF.Identity, scale=-1.0,
                     inc=True, mark=("cnt_done", tile), w=w)
                sst["i"] += 1
                sst["it"] = -1

    def barrier_cb(Eng_):
        Eng_.need(w, ("spl", 2 * NIC - 1))
        Eng_.need(w, ("spl", 2 * NIC - 2))
        Eng_.need(w, ("spl_a", NIC - 1))
        Eng_.need(w, ("spl_a", NIC - 2))
        Eng_.need(w, ("grp", 2 * NIC - 1, T - 1))
        Eng_.need(w, ("ev", 2 * NIC * T - 1))

    def emit_piece_work(t):
        PBL = d["PB"]
        for g in range(len(PBL)):
            pg = t * len(PBL) + g
            par = pg % 2
            nb = PBL[g]
            k0 = sum(PBL[:g]) * 128
            kw_ = nb * 128
            if pg >= 2:
                Y.need(w, ("pb", pg - 2))
            else:
                barrier_cb(Y)
            Y.need(w, ("cnt_done", t))
            Y.op("dma_start", out=t_["spc"][par][:, 0:kw_],
                 in_=t_["sp_s"][t, :, k0:k0 + kw_],
                 inc=[(sm["sPP"][par], 16)])
            Y.op("dma_start", out=t_["upc"][par][:, 0:kw_],
                 in_=t_["sp_u"][t, :, k0:k0 + kw_],
                 inc=[(sm["sPP"][par], 16)], mark=("ppl", pg), w=w)
            V.need(w, ("ppl", pg))
            V.need(w, ("cnt_done", t))
            V.chain()
            V.op("scalar_tensor_tensor", out=t_["abs16"][:, 0:kw_],
                 in0=t_["spc"][par][:, 0:kw_], scalar=-1.0,
                 in1=t_["spc"][par][:, 0:kw_],
                 op0=ALU.mult, op1=ALU.max, inc=True)
            V.chain()
            V.op("scalar_tensor_tensor", out=t_["ms16"][:, 0:kw_],
                 in0=t_["abs16"][:, 0:kw_],
                 scalar=t_["tfinT"][:, t:t + 1],
                 in1=t_["spc"][par][:, 0:kw_],
                 op0=ALU.is_ge, op1=ALU.mult, inc=True)
            V.chain()
            if pg >= 2:
                V.need(w, ("ptTp", pg - 2))
            V.op("tensor_tensor", out=t_["p16"][par][:, 0:kw_],
                 in0=t_["ms16"][:, 0:kw_], in1=t_["upc"][par][:, 0:kw_],
                 op=ALU.mult, inc=True, mark=("pb", pg), w=w)
            P.need(w, ("pb", pg))
            nsg = (nb + 7) // 8
            for sg in range(nsg):
                gsg = gsgctr[0]
                gsgctr[0] += 1
                sgw = min(8, nb - sg * 8)
                pst = t_["pstr"][gsg % 2]
                if gsg >= 2:
                    P.need(w, ("ptev", gsg - 2))
                for k in range(sgw):
                    kb_ = sg * 8 + k
                    P.op("transpose", pst[:, k, :],
                         t_["p16"][par][:, kb_ * 128:(kb_ + 1) * 128],
                         t_["ident_sb"][:],
                         inc=(True if k == sgw - 1 else None),
                         mark=(("ptT", gsg) if k == sgw - 1 else None), w=w)
                if sg == nsg - 1:
                    w.prod[("ptTp", pg)] = w.prod[("ptT", gsg)]
                S.need(w, ("ptT", gsg))
                if pg >= 2:
                    S.need(w, ("pt_spl", pg - 2))
                S.op("copy", out=t_["ptstage"][par][:, sg * 8:sg * 8 + sgw, :],
                     in_=pst[:, 0:sgw, :], inc=True,
                     mark=(("ptev", gsg)), w=w)
            G.need(w, ("ptev", gsgctr[0] - 1))
            G.op("dma_start",
                 out=t_["sp_pt"][t, :, k0:k0 + kw_],
                 in_=t_["ptstage"][par][:, 0:nb, :],
                 inc=[(sm["sPTW"][par], 16)], mark=("pt_spl", pg), w=w)

    gsgctr = [0]

    b_phase(1)
    b_phase(2)
    while vst["i"] < len(vtiles) or sst["i"] < len(stiles):
        v_steps(NIT)
        s_steps(NIT)

    for eb in range(ERB):
        par = 0
        if eb >= 1:
            G.need(w, ("hT_mm", eb - 1, T - 1))
        else:
            barrier_cb(G)
        G.op("dma_start", out=t_["ldc"][par][:, :, :],
             in_=t_["ldT"].rearrange("(hb p) r -> p hb r", p=128)[:, :, eb * 128:(eb + 1) * 128],
             inc=[(sm["sLD"][par], 16)], mark=("ld", eb), w=w)
        for t in range(T):
            P.need(w, ("ld", eb))
            if eb == 0 and t == 0:
                P.need(w, ("ev", 2 * NIC * T - 1))
            if eb > 0 or t > 0:
                P.need(w, ("hT_ev", eb - 1 if t == 0 else eb,
                           T - 1 if t == 0 else t - 1))
            for hb in range(HB):
                P.op("matmul", t_["psB"][0][:, 0:128],
                     t_["ldc"][par][:, hb, :],
                     t_["xnT"][:, hb, t * 128:(t + 1) * 128],
                     start=(hb == 0), stop=(hb == HB - 1),
                     inc=(True if hb == HB - 1 else None),
                     mark=(("hT_mm", eb, t) if hb == HB - 1 else None), w=w)
            S.need(w, ("hT_mm", eb, t))
            S.need(w, ("coefEB",))
            S.op("activation", out=t_["hpT16"][:, t, eb, :],
                 in_=t_["psB"][0][:, 0:128], func=AF.Identity,
                 scale=t_["coefEB"][:, eb:eb + 1], inc=True,
                 mark=("hT_ev", eb, t), w=w)
    for t in range(T):
        w.prod[("hpT", t)] = w.prod[("hT_ev", ERB - 1, t)]

    for t in range(T):
        emit_piece_work(t)

    NPP = len(d["PB"])

    def barrier_c2(Eng_):
        Eng_.need(w, ("ptev", gsgctr[0] - 1))
        Eng_.need(w, ("ptev", gsgctr[0] - 2))
        Eng_.need(w, ("pb", T * NPP - 1))
        for i in range(2):
            Eng_.need(w, ("pt_spl", T * NPP - 1 - i))
            Eng_.need(w, ("ppl", T * NPP - 1 - i))
        Eng_.need(w, ("hT_mm", ERB - 1, T - 1))
        Eng_.need(w, ("hT_ev", ERB - 1, T - 1))
        Eng_.need(w, ("ld", ERB - 1))

    for hc in range(NHC):
        h0 = hc * 512
        for half in range(2):
            gidx = hc * 2 + half
            hb0 = half * (HB // 2)
            if gidx >= 2:
                G.need(w, ("c2base", hc - 1, T - 1))
            else:
                barrier_c2(G)
            G.op("dma_start", out=t_["woc"][gidx % 2][:, :, :],
                 in_=t_["woT"].rearrange("(hb p) o -> p hb o", p=128)[:, hb0:hb0 + HB // 2,
                                                                     h0:h0 + 512],
                 inc=[(sm["sWO"][gidx % 2], 16)], mark=("wo", hc, half), w=w)
        if hc >= 2:
            G.need(w, ("c2delta", hc - 2, T - 1))
        G.op("dma_start", out=t_["luc"][hc % 2][:, :, :],
             in_=t_["luT"].rearrange("(eb p) o -> p eb o", p=128)[:, :, h0:h0 + 512],
             inc=[(sm["sLU"][hc % 2], 16)], mark=("lu", hc), w=w)

        for t in range(T):
            bank = t_["psC"][t]
            if hc >= 1:
                P.need(w, ("c2ev", hc - 1, t))
            elif t >= 4:
                P.need(w, ("ptev", gsgctr[0] - 1))
            else:
                P.need(w, ("hT_ev", ERB - 1, T - 1))
                P.need(w, ("ev", 2 * NIC * T - 1))
            for hb in range(HB):
                half = hb // (HB // 2)
                P.need(w, ("wo", hc, half))
                P.op("matmul", bank[:, :],
                     t_["xnT"][:, hb, t * 128:(t + 1) * 128],
                     t_["woc"][(hc * 2 + half) % 2][:, hb - half * (HB // 2), :],
                     start=(hb == 0), stop=False,
                     inc=(True if hb == HB - 1 else None),
                     mark=(("c2base", hc, t) if hb == HB - 1 else None), w=w)
            P.need(w, ("lu", hc))
            P.need(w, ("hpT", t))
            for b in range(ERB):
                P.op("matmul", bank[:, :], t_["hpT16"][:, t, b, :],
                     t_["luc"][hc % 2][:, b, :], start=False, stop=False,
                     inc=(True if b == ERB - 1 else None),
                     mark=(("c2delta", hc, t) if b == ERB - 1 else None), w=w)
        for g in range(NKG):
            kg = KGW[g]
            seq = hc * NKG + g
            par = seq % 2
            if seq >= 2:
                Y.need(w, ("c2kg", seq - 2))
                G.need(w, ("c2kg", seq - 2))
            else:
                barrier_c2(Y)
            Y.op("dma_start", out=t_["ptc"][par][:, :, 0:kg, :],
                 in_=t_["sp_pt"].rearrange("t p kc -> p t kc")[:, :, g * KGS * 128:(g * KGS + kg) * 128]
                 .rearrange("p t (k c) -> p t k c", c=128),
                 inc=[(sm["sPTR"][par], 16)], mark=("ptr", hc, g), w=w)
            G.op("dma_start", out=t_["wdc"][par][:, 0:kg, :],
                 in_=t_["wdT"].rearrange("(k p) o -> p k o", p=128)[:, g * KGS:g * KGS + kg,
                                                                   h0:h0 + 512],
                 inc=[(sm["sWD"][par], 16)], mark=("wd", hc, g), w=w)
            P.need(w, ("ptr", hc, g))
            P.need(w, ("wd", hc, g))
            last_g = (g == NKG - 1)
            for k in range(kg):
                for t in range(T):
                    P.op("matmul", t_["psC"][t][:, :],
                         t_["ptc"][par][:, t, k, :], t_["wdc"][par][:, k, :],
                         start=False, stop=(last_g and k == kg - 1),
                         inc=(True if (k == kg - 1 and t == T - 1) else None),
                         mark=(("c2kg", seq) if (k == kg - 1 and t == T - 1)
                               else None), w=w)
        for t in range(T):
            par2 = (hc * T + t) % 2
            et = hc * T + t
            if et >= 2:
                Y.need(w, ("c2ev", hc, t - 2) if t >= 2 else
                       ("c2ev", hc - 1, T - 2 + t))
            else:
                barrier_c2(Y)
            Y.op("dma_start", out=t_["xstage"][par2][:, :],
                 in_=t_["x_own"][t * 128:(t + 1) * 128, h0:h0 + 512],
                 inc=[(sm["sXC"][par2], 16)], mark=("xc2", hc, t), w=w)
            V.need(w, ("c2kg", hc * NKG + NKG - 1))
            V.need(w, ("xc2", hc, t))
            if et >= 2:
                V.need(w, ("ow", hc, t - 2) if t >= 2 else
                       ("ow", hc - 1, T - 2 + t))
            V.chain()
            V.op("tensor_tensor", out=t_["ostage"][par2][:, :],
                 in0=t_["xstage"][par2][:, :], in1=t_["psC"][t][:, :],
                 op=ALU.add, inc=True, mark=("c2ev", hc, t), w=w)
            Y.need(w, ("c2ev", hc, t))
            Y.op("dma_start", out=t_["out"][t * 128:(t + 1) * 128, h0:h0 + 512],
                 in_=t_["ostage"][par2][:, :],
                 inc=[(sm["sOW"][par2], 16)], mark=("ow", hc, t), w=w)

    if d.get("DEBUG"):
        Y.need(w, ("cnt_done", T - 1))
        Y.op("dma_start", out=t_["dbg_thr"][:, :], in_=t_["tfinT"][:, :],
             inc=[(sm["sAR"], 16)])
        Y.wait(sm["sAR"], 16 * (T + 1))
    tot = NHC * T
    Y.wait(sm["sOW"][0], 16 * ((tot + 1) // 2))
    Y.wait(sm["sOW"][1], 16 * (tot // 2))


def build_nc(cfg):
    d = derive(cfg)
    T, HB, NKG, NHC, ERB = d["T"], d["HB"], d["NKG"], d["NHC"], d["ERB"]
    H, I, E, RK, K, ICH, KGS = (d["H"], d["I"], d["E"], d["RK"], d["K"],
                                d["ICH"], d["KGS"])
    ER = d["ER"]
    ROWS = d["ROWS"]

    nc = bass.Bass("TRN2", target_bir_lowering=False)
    t_ = {}
    t_["x_own"] = nc.dram_tensor("x_own", [ROWS, H], F32, kind="ExternalInput").ap()
    t_["x_sib"] = nc.dram_tensor("x_sib", [ROWS, H], F32, kind="ExternalInput").ap()
    t_["wgT"] = nc.dram_tensor("wgT", [H, I], F32, kind="ExternalInput").ap()
    t_["wuT"] = nc.dram_tensor("wuT", [H, I], F32, kind="ExternalInput").ap()
    t_["wdT"] = nc.dram_tensor("wdT", [I, H], F32, kind="ExternalInput").ap()
    t_["woT"] = nc.dram_tensor("woT", [H, H], F32, kind="ExternalInput").ap()
    t_["ldT"] = nc.dram_tensor("ldT", [H, ER], F32, kind="ExternalInput").ap()
    t_["luT"] = nc.dram_tensor("luT", [ER, H], F32, kind="ExternalInput").ap()
    t_["wqT"] = nc.dram_tensor("wqT", [H, K], F32, kind="ExternalInput").ap()
    t_["keysT"] = nc.dram_tensor("keysT", [K, E], F32, kind="ExternalInput").ap()
    t_["ident"] = nc.dram_tensor("ident", [128, 128], F16, kind="ExternalInput").ap()
    t_["onesr"] = nc.dram_tensor("onesr", [1, 128], F16, kind="ExternalInput").ap()
    t_["onesc"] = nc.dram_tensor("onesc", [128, 1], F16, kind="ExternalInput").ap()
    t_["out"] = nc.dram_tensor("out", [ROWS, H], F32, kind="ExternalOutput").ap()
    spk = "ExternalOutput" if cfg.get("DEBUG") else "Internal"
    t_["sp_s"] = nc.dram_tensor("sp_s", [T, 128, I], F16, kind=spk).ap()
    t_["sp_a"] = nc.dram_tensor("sp_a", [T, 128, I], F16, kind=spk).ap()
    t_["sp_u"] = nc.dram_tensor("sp_u", [T, 128, I], F16, kind=spk).ap()
    t_["sp_pt"] = nc.dram_tensor("sp_pt", [T, 128, d["KB"] * 128], F16,
                                 kind=spk).ap()
    if cfg.get("DEBUG"):
        t_["dbg_thr"] = nc.dram_tensor("dbg_thr", [128, T], F32,
                                       kind="ExternalOutput").ap()

    stack_pers = ExitStack()
    stack_A = ExitStack()
    stack_B = ExitStack()
    stack_C1a = ExitStack()
    stack_C1b = ExitStack()
    stack_C2 = ExitStack()
    ep = stack_pers.enter_context

    t_["xnT"] = ep(nc.sbuf_tensor("xnT", [128, HB, T * 128], F16))
    t_["ident_sb"] = ep(nc.sbuf_tensor("ident_sb", [128, 128], F16))
    t_["onesr_sb"] = ep(nc.sbuf_tensor("onesr_sb", [1, 128], F16))
    t_["ones_col16"] = ep(nc.sbuf_tensor("ones_col16", [128, 1], F16))
    t_["keysT_sb"] = ep(nc.sbuf_tensor("keysT_sb", [128, E], F16))
    t_["coefbc"] = ep(nc.sbuf_tensor("coefbc", [128, E], F32))
    t_["coefEB"] = ep(nc.sbuf_tensor("coefEB", [128, ERB], F32))
    t_["tfinT"] = ep(nc.sbuf_tensor("tfinT", [128, T], F32))
    t_["epsb"] = ep(nc.sbuf_tensor("epsb", [128, 1], F32))
    for nm in ["cb", "geb", "db", "ntS", "Ssum"]:
        t_[nm] = ep(nc.sbuf_tensor(nm, [128, 1], F32))
    for nm in ["msum", "mrow", "ssq", "sq", "rstd"]:
        t_[nm] = [ep(nc.sbuf_tensor(nm + str(i), [128, 1], F32))
                  for i in range(2)]
    t_["lob"] = [ep(nc.sbuf_tensor(f"lob{i}", [128, 1], F32)) for i in range(2)]
    t_["cm8"] = ep(nc.sbuf_tensor("cm8", [128, 1], F32))
    t_["negt"] = [ep(nc.sbuf_tensor(f"negt{i}", [128, 1], F32)) for i in range(2)]
    t_["nh"] = [ep(nc.sbuf_tensor(f"nh{i}", [128, 1], F32)) for i in range(2)]
    t_["sgb"] = ep(nc.sbuf_tensor("sgb", [128, 1], F32))
    t_["Ssum2"] = ep(nc.sbuf_tensor("Ssum2", [128, 1], F32))
    t_["tb"] = [ep(nc.sbuf_tensor(f"tbs{i}", [128, 1], F32)) for i in range(2)]
    for nm, shp in [("smax", [1, 1]), ("nsmax", [1, 1]), ("ssum1", [1, 1]),
                    ("rsum", [1, 1]), ("scores", [1, E]), ("esc", [1, E])]:
        t_[nm] = ep(nc.sbuf_tensor(nm, shp, F32))
    t_["coef16"] = ep(nc.sbuf_tensor("coef16", [1, E], F16))
    t_["q16"] = ep(nc.sbuf_tensor("q16", [1, K], F16))
    t_["q16T"] = ep(nc.sbuf_tensor("q16T", [128, 1], F16))
    t_["meanT16"] = ep(nc.sbuf_tensor("meanT16", [128, HB], F16))

    ea = stack_A.enter_context
    t_["xA"] = [ea(nc.sbuf_tensor(f"xA{i}", [128, H], F32)) for i in range(2)]
    t_["xc"] = [ea(nc.sbuf_tensor(f"xc{i}", [128, H], F32)) for i in range(2)]
    t_["scr4k"] = ea(nc.sbuf_tensor("scr4k", [128, H], F16))
    t_["xn16"] = [ea(nc.sbuf_tensor(f"xn16{i}", [128, H], F16)) for i in range(2)]
    t_["colsb"] = ea(nc.sbuf_tensor("colsb", [1, H], F32))
    t_["mean16"] = ea(nc.sbuf_tensor("mean16", [1, H], F16))
    t_["wqT_sb"] = ea(nc.sbuf_tensor("wqT_sb", [128, HB, K], F16))
    stack_A.close()

    eb_ = stack_B.enter_context
    t_["wch"] = [eb_(nc.sbuf_tensor(f"wch{i}", [128, HB, ICH], F16))
                 for i in range(2)]
    t_["sstage"] = [eb_(nc.sbuf_tensor(f"sstage{i}", [128, T, ICH], F16))
                    for i in range(2)]
    stack_B1x = ExitStack()
    t_["astage"] = [stack_B1x.enter_context(
        nc.sbuf_tensor(f"astage{i}", [128, T, ICH], F16)) for i in range(2)]
    stack_B1x.close()

    ec = stack_C1a.enter_context
    t_["a16b"] = ec(nc.sbuf_tensor("a16b", [128, I], F16, side="right"))
    t_["scrV"] = ec(nc.sbuf_tensor("scrV", [128, I], U8, side="right"))
    t_["scrS"] = ec(nc.sbuf_tensor("scrS", [128, I // 2], U8, side="right"))
    t_["a16c"] = ec(nc.sbuf_tensor("a16c", [128, I], F16, side="right"))

    stack_B.close()

    stack_hp = ExitStack()
    t_["hpT16"] = stack_hp.enter_context(
        nc.sbuf_tensor("hpT16", [128, T, ERB, 128], F16))


    e1 = stack_C1b.enter_context
    pcw = d["PBMAX"] * 128
    t_["spc"] = [e1(nc.sbuf_tensor(f"spc{i}", [128, pcw], F16)) for i in range(2)]
    t_["abs16"] = e1(nc.sbuf_tensor("abs16", [128, pcw], F16))
    t_["upc"] = [e1(nc.sbuf_tensor(f"upc{i}", [128, pcw], F16)) for i in range(2)]
    t_["ms16"] = e1(nc.sbuf_tensor("ms16", [128, pcw], F16))
    t_["p16"] = [e1(nc.sbuf_tensor(f"p16{i}", [128, pcw], F16)) for i in range(2)]
    t_["ptstage"] = [e1(nc.sbuf_tensor(f"ptstage{i}", [128, d["PBMAX"], 128], F16))
                     for i in range(2)]
    t_["ldc"] = [e1(nc.sbuf_tensor("ldc0", [128, HB, 128], F16))]
    stack_C1b.close()
    stack_C1a.close()

    e2 = stack_C2.enter_context
    t_["woc"] = [e2(nc.sbuf_tensor(f"woc{i}", [128, HB // 2, 512], F16))
                 for i in range(2)]
    t_["luc"] = [e2(nc.sbuf_tensor(f"luc{i}", [128, ERB, 512], F16))
                 for i in range(2)]
    t_["wdc"] = [e2(nc.sbuf_tensor(f"wdc{i}", [128, KGS, 512], F16))
                 for i in range(2)]
    t_["ptc"] = [e2(nc.sbuf_tensor(f"ptc{i}", [128, T, KGS, 128], F16))
                 for i in range(2)]
    t_["xstage"] = [e2(nc.sbuf_tensor(f"xstage{i}", [128, 512], F32))
                    for i in range(2)]
    t_["ostage"] = [e2(nc.sbuf_tensor(f"ostage{i}", [128, 512], F32))
                    for i in range(2)]
    stack_C2.close()

    ps_A = ExitStack()
    ps_pstr = ExitStack()
    ps_B = ExitStack()
    ps_C47 = ExitStack()
    t_["pstr"] = [ps_pstr.enter_context(
        nc.psum_tensor(f"pstr{i}", [128, 8, 128], F16, side="right"))
        for i in range(2)]
    pa = ps_A.enter_context
    t_["colps"] = [pa(nc.psum_tensor(f"colps{i}", [1, 512], F32))
                   for i in range(2)]
    psMf16 = pa(nc.psum_tensor("psMf16", [128, HB, 2], F16))
    t_["psq"] = psMf16
    t_["psq2"] = psMf16[:, 0, 0:1]
    psMf32 = pa(nc.psum_tensor("psMf32", [128, K + 2 * E], F32))
    t_["qps"] = psMf32[:, 0:K]
    t_["scps"] = psMf32[:, K:K + E]
    t_["cbps"] = psMf32[:, K + E:K + 2 * E]
    ps_A.close()
    t_["psB"] = [ps_B.enter_context(nc.psum_tensor(f"psB{i}", [128, 512], F32))
                 for i in range(4)]
    ps_pstr.close()
    t_["psC"] = list(t_["psB"])
    if T > 4:
        t_["psC"] = list(t_["psB"]) + [ps_C47.enter_context(
            nc.psum_tensor(f"psC{i}", [128, 512], F32, side="right"))
            for i in range(T - 4)]
    else:
        t_["psC"] = t_["psB"][:T]

    scalar_sems = {}
    list_sems = {}
    with ExitStack() as sem_stack:
        es = sem_stack.enter_context
        for nm in ["cV", "cS", "cT", "cY", "cG", "sAux", "sAuxG", "sAR",
                   "sAR2"]:
            scalar_sems[nm] = es(nc.semaphore(nm))
        for nm in ["sXA", "sWG", "sWU", "sSpS", "sSpA", "sSpU",
                   "sPTR", "sWD", "sWO", "sLU", "sLD", "sXC", "sOW"]:
            list_sems[nm] = [es(nc.semaphore(nm + str(i))) for i in range(2)]
        list_sems["sPTW"] = [es(nc.semaphore("sPTW" + str(i)))
                             for i in range(derive(cfg)["PD"])]
        list_sems["sPP"] = [es(nc.semaphore("sPP" + str(i)))
                            for i in range(derive(cfg)["PD"])]

        with nc.Block() as block:
            def run(active, engines):
                wlk = Walk(scalar_sems, list_sems, active, engines)
                schedule(wlk, t_, d)

            @block.sync
            def _(sync):
                run("sync", {"sync": sync})

            @block.vector
            def _(vector):
                run("vector", {"vector": vector})

            @block.scalar
            def _(scalar):
                run("scalar", {"scalar": scalar})

            @block.tensor
            def _(tensor):
                run("tensor", {"tensor": tensor})

            @block.gpsimd
            def _(gpsimd):
                run("gpsimd", {"gpsimd": gpsimd})

    return nc


def _prep_weights(w_gate, w_up, w_down, W_org, lora_down, lora_up, keys, Wq):
    H = W_org.shape[1]
    wgT = np.ascontiguousarray(np.asarray(w_gate, np.float32).T)
    wuT = np.ascontiguousarray(np.asarray(w_up, np.float32).T)
    wdT = np.ascontiguousarray(np.asarray(w_down, np.float32).T)
    woT = np.ascontiguousarray(np.asarray(W_org, np.float32).T)
    ldT = np.ascontiguousarray(
        np.asarray(lora_down, np.float32).transpose(1, 0, 2).reshape(H, -1))
    luT = np.ascontiguousarray(np.asarray(lora_up, np.float32).reshape(-1, H))
    wqT = np.ascontiguousarray(np.asarray(Wq, np.float32).T)
    keysT = np.ascontiguousarray(np.asarray(keys, np.float32).T)
    return wgT, wuT, wdT, woT, ldT, luT, wqT, keysT


def _mk_map(x_own, x_sib, wgT, wuT, wdT, woT, ldT, luT, wqT, keysT):
    return {
        "x_own": np.ascontiguousarray(x_own, dtype=np.float32),
        "x_sib": np.ascontiguousarray(x_sib, dtype=np.float32),
        "wgT": wgT, "wuT": wuT, "wdT": wdT, "woT": woT,
        "ldT": ldT, "luT": luT, "wqT": wqT, "keysT": keysT,
        "ident": np.eye(128, dtype=np.float16),
        "onesr": np.ones((1, 128), dtype=np.float16),
        "onesc": np.ones((128, 1), dtype=np.float16),
    }


def kernel(L0_attn_x, L0_attn_w_gate, L0_attn_w_up, L0_attn_w_down,
           L0_ffn_W_org, L0_ffn_lora_down, L0_ffn_lora_up, L0_ffn_lora_keys,
           L0_ffn_Wq):
    cfg = FULL_CFG
    x = np.asarray(L0_attn_x, dtype=np.float32)
    B, Sq, H = x.shape
    wgT, wuT, wdT, woT, ldT, luT, wqT, keysT = _prep_weights(
        L0_attn_w_gate, L0_attn_w_up, L0_attn_w_down, L0_ffn_W_org,
        L0_ffn_lora_down, L0_ffn_lora_up, L0_ffn_lora_keys, L0_ffn_Wq)
    nc = build_nc(cfg)
    RW = cfg["ROWS"]
    in_maps = []
    for c in range(8):
        b, hh = c // 2, c % 2
        in_maps.append(_mk_map(x[b, hh * RW:(hh + 1) * RW, :],
                               x[b, (1 - hh) * RW:(2 - hh) * RW, :],
                               wgT, wuT, wdT, woT, ldT, luT, wqT, keysT))
    res = run_bass_kernel_spmd(nc, in_maps, core_ids=list(range(8)))
    out = np.empty((B, Sq, H), dtype=np.float32)
    for c in range(8):
        b, hh = c // 2, c % 2
        out[b, hh * RW:(hh + 1) * RW, :] = res.results[c]["out"]
    return out
